# version 17
# baseline (speedup 1.0000x reference)
import numpy as np
import concourse.bass as bass
import concourse.mybir as mybir
from concourse.bass_utils import run_bass_kernel_spmd

F32 = mybir.dt.float32
BF16 = mybir.dt.bfloat16
I32 = mybir.dt.int32
AF = mybir.ActivationFunctionType
ALU = mybir.AluOpType
AXX = mybir.AxisListType.X

DEBUG = False
NCORES = 8
B, S, D = 2, 4096, 2048
H, DH = 16, 128
FI = 8192
T = 1024
NT = T // 128
SHD = 4
RSQ = float(1.0 / np.sqrt(128.0))
ITB = 4
NITB = (FI // 128) // ITB


class Plan:

    def __init__(self):
        self.q = {k: [] for k in ("sync", "gpsimd", "vector", "scalar", "tensor")}
        self.cnt = {}

    def sem(self, name):
        self.cnt.setdefault(name, 0)

    def val(self, name):
        return self.cnt[name]

    def op(self, eng, fn, inc=None):
        if eng == "vector" or eng == "scalar":
            auto = "sv" if eng == "vector" else "ss"
            prev = self.cnt[auto]
            if prev > 0:
                self.q[eng].append(
                    lambda e, s, sem=auto, v=prev: e.wait_ge(s[sem], v))
            self.cnt[auto] += 1
            self.q[eng].append(
                lambda e, s, sem=auto: fn(e, s).then_inc(s[sem], 1))
            return self.cnt[auto]
        if inc is None:
            self.q[eng].append(lambda e, s: fn(e, s))
            return None
        self.cnt[inc] += 1
        self.q[eng].append(lambda e, s, inc=inc: fn(e, s).then_inc(s[inc], 1))
        return self.cnt[inc]

    def dma(self, eng, out, in_, sem, slow=False):
        self.cnt[sem] += 16
        self.q[eng].append(lambda e, s, out=out, in_=in_, sem=sem, slow=slow:
                           e.dma_start(out=out, in_=in_,
                                       allow_slow_non_contiguous=slow)
                           .then_inc(s[sem], 16))
        return self.cnt[sem]

    def wait(self, eng, sem, v=None):
        if v is None:
            v = self.cnt[sem]
        if v <= 0:
            return
        self.q[eng].append(lambda e, s, sem=sem, v=v: e.wait_ge(s[sem], v))


def build_nc():
    nc = bass.Bass("TRN2", target_bir_lowering=False, debug=False)

    din = lambda n, sh, dt: nc.dram_tensor(n, sh, dt, kind="ExternalInput").ap()
    x_in = din("x", [T, D], F32)
    cq, ck, cv, co = (din(n, [D, D], F32) for n in ("cq", "ck", "cv", "co"))
    sq, sk, svs = (din(n, [64, D], F32) for n in ("sq", "sk", "svs"))
    so = din("so", [D, D], F32)
    bq, bk, bv, bo = (din(n, [D], F32) for n in ("bq", "bk", "bv", "bo"))
    bqcd = din("bqcd", [128, H], F32)
    bkcd = din("bkcd", [128, H], F32)
    bvcd = din("bvcd", [128, H], F32)
    gscd = din("gscd", [128, FI // 128], F32)
    uscd = din("uscd", [128, FI // 128], F32)
    wgT = din("wgT", [D, FI], I32)
    wuT = din("wuT", [D, FI], I32)
    wdT = din("wdT", [FI, D], I32)
    gsc = din("gsc", [FI], F32)
    usc = din("usc", [FI], F32)
    dsc = din("dsc", [D], F32)
    y_out = nc.dram_tensor("y", [T, D], F32, kind="ExternalOutput").ap()
    if DEBUG:
        dxn = nc.dram_tensor("dxn", [128, H * T], BF16, kind="ExternalOutput").ap()
        dqt = nc.dram_tensor("dqt", [128, H * T], BF16, kind="ExternalOutput").ap()
        dkk = nc.dram_tensor("dkk", [SHD * H * DH, T], BF16, kind="ExternalOutput").ap()
        dkv = nc.dram_tensor("dkv", [SHD * H * T, DH], BF16, kind="ExternalOutput").ap()
        dkn = nc.dram_tensor("dkn", [SHD * H, 1], F32, kind="ExternalOutput").ap()
        dcs = nc.dram_tensor("dcs", [128, H], F32, kind="ExternalOutput").ap()
        dao = nc.dram_tensor("dao", [128, H * T], BF16, kind="ExternalOutput").ap()
        dfa = nc.dram_tensor("dfa", [128, NT * D], F32, kind="ExternalOutput").ap()
        dnr = nc.dram_tensor("dnr", [1, 3 * H + SHD * H], F32, kind="ExternalOutput").ap()

    kvk_in = nc.dram_tensor("kvk_in", [H * DH, T], BF16)
    kvv_in = nc.dram_tensor("kvv_in", [H * T, DH], BF16)
    kvn_in = nc.dram_tensor("kvn_in", [H, 1], F32)
    kvk_out = nc.dram_tensor("kvk_out", [SHD * H * DH, T], BF16)
    kvv_out = nc.dram_tensor("kvv_out", [SHD * H * T, DH], BF16)
    kvn_out = nc.dram_tensor("kvn_out", [SHD * H, 1], F32)
    import numpy as _np
    import ml_dtypes as _mld
    _ex = _np.zeros((4, 128), _np.float32)
    for _r in range(4):
        _ex[_r, _r * 32:(_r + 1) * 32] = 1.0
    ex_dram = nc.inline_tensor(_ex.astype(_mld.bfloat16), name="ex4")

    from contextlib import ExitStack
    with ExitStack() as ctx:
        sb = lambda n, shp, dt: ctx.enter_context(nc.sbuf_tensor(n, shp, dt))
        xnT = sb("xnT", [128, H * T], BF16)
        aoT = xnT
        QT = sb("QT", [128, H * T], BF16)
        facc = sb("facc", [128, NT * D], F32)
        wAr = sb("wAr", [128, 16384], BF16)
        wc = wAr[:, 0:8192]
        sexp = wAr[:, 8192:16384]
        gw = wAr[:, 0:8192]
        uw = wAr[:, 8192:16384]
        woc = wAr[:, 0:16384]
        dAr = sb("dAr", [128, (ITB + 2) * D], BF16)
        dwb = dAr[:, :]
        kbuf = dAr[:, 0:2048]
        vbf = dAr[:, 2048:4096]
        ptb = dAr[:, 4096:6144]
        xbuf = dAr[:, 6144:10240].bitcast(F32)
        aAr = sb("aAr", [128, 2 * D], BF16)
        xnstg = aAr[:, :]
        vstg = aAr[:, 0:2048]
        vts = aAr[:, 2048:4096]
        hT = aAr[:, :]
        ksq0 = aAr[:, 0:1024]
        ksq1 = aAr[:, 1024:2048]
        sqs = None
        kstg = sb("kstg", [128, 2 * T], BF16)
        rbs = kstg[:, :].bitcast(F32)[:, 0:1024]
        outs = aAr[:, :].bitcast(F32)[:, 0:1024]
        ident = sb("ident", [128, 128], BF16)
        exw = sb("exw", [4, 128], BF16)
        ones_b = sb("ones_b", [128, 1], BF16)
        ones_f = sb("ones_f", [1, 128], F32)
        bqc = sb("bqc", [128, H], F32)
        bkc = sb("bkc", [128, H], F32)
        bvc = sb("bvc", [128, H], F32)
        gscol = sb("gscol", [128, FI // 128], F32)
        uscol = sb("uscol", [128, FI // 128], F32)
        hTx = sb("hTx", [128, 2048], BF16)
        rowA = sb("rowA", [1, T], F32)
        dsrow = rowA[0:1, 0:512]
        rrow = rowA[0:1, 0:T]
        bor = rowA[0:1, 0:512]
        dsb = wAr[:, 0:4096].bitcast(F32)
        csb = sb("csb", [128, H], F32)
        knr = sb("knr", [1, SHD * H], F32)
        knc = sb("knc", [SHD * H, 1], F32)
        mq2 = sb("mq2", [1, H], F32)
        kmx = sb("kmx", [1, H], F32)
        mk2 = sb("mk2", [1, H], F32)
        crow = sb("crow", [1, H], F32)
        nrm_a = sb("nrm_a", [1, 2], F32)
        tmp1 = sb("tmp1", [128, 2], F32)
        tmp2 = sb("tmp2", [128, 2], F32)
        tmp3 = sb("tmp3", [128, 2], F32)

        pP0 = ctx.enter_context(nc.psum_tensor("pP0", [128, 1024], F32))
        pP1 = ctx.enter_context(nc.psum_tensor("pP1", [128, 1024], F32))
        pP2 = ctx.enter_context(nc.psum_tensor("pP2", [128, 1024], F32))
        pR0 = ctx.enter_context(nc.psum_tensor("pR0", [128, 512], F32))
        pR1 = ctx.enter_context(nc.psum_tensor("pR1", [128, 512], F32))
        pR0b = pR0[:, :].bitcast(BF16)
        pR1b = pR1[:, :].bitcast(BF16)

        sem_names = ["ldx", "lx1", "ldc", "ldg", "lw0", "lw1", "kv0", "kv1",
                     "sk0", "sk1", "skn", "knl", "st0", "st1", "dbg", "cc", "sv", "ss", "st", "sg",
                     "srA", "srB"]
        sems = {n: ctx.enter_context(nc.semaphore(n)) for n in sem_names}
        pl = Plan()
        for n in sem_names:
            pl.sem(n)
        sqs = facc[:, 0:D]

        pl.dma("gpsimd", exw[:, :], ex_dram[:, :], "ldg")
        pl.dma("gpsimd", bqc[:, :], bqcd, "ldg")
        pl.dma("gpsimd", bkc[:, :], bkcd, "ldg")
        pl.dma("gpsimd", bvc[:, :], bvcd, "ldg")
        pl.dma("gpsimd", gscol[:, :], gscd, "ldg")
        pl.dma("gpsimd", uscol[:, :], uscd, "ldg")
        tok_const = pl.val("ldg")
        pl.op("gpsimd", lambda e, s: e.memset(ident[:, :], 0.0), inc="sg")
        pl.wait("gpsimd", "sg", pl.val("sg"))
        pl.op("gpsimd", lambda e, s: e.affine_select(
            out=ident[:, :], in_=ident[:, :], compare_op=ALU.not_equal,
            fill=1.0, base=0, pattern=[[-1, 128]], channel_multiplier=1), inc="sg")
        pl.op("vector", lambda e, s: e.memset(ones_b[:, :], 1.0))
        pl.op("vector", lambda e, s: e.memset(ones_f[:, :], 1.0), inc="sv")
        v_ones = pl.val("sv")
        pl.wait("tensor", "sv", v_ones)
        pl.wait("tensor", "ldg", tok_const)
        pl.wait("vector", "ldg", tok_const)
        pl.wait("scalar", "ldg", tok_const)

        ln_done = {}
        tr_last = {}
        tr_evac = []
        for tt in range(NT):
            sl = tt % 2
            xb = xbuf[:, 0:D] if sl == 0 else facc[:, D:2 * D]
            xs = xnstg[:, sl * D:(sl + 1) * D]
            tmp1s = tmp1[:, sl:sl + 1]
            tmp2s = tmp2[:, sl:sl + 1]
            tmp3s = tmp3[:, sl:sl + 1]
            if tt >= 2:
                pl.wait("sync", "sv", ln_done[tt - 2])
            lxs = "ldx" if sl == 0 else "lx1"
            pl.dma("sync", xb, x_in[tt * 128:(tt + 1) * 128, :], lxs)
            t_ld = pl.val(lxs)
            pl.wait("vector", lxs, t_ld)
            pl.op("vector", lambda e, s, xb=xb, tmp1s=tmp1s: e.reduce_sum(
                tmp1s, xb, axis=AXX), inc="sv")
            pl.wait("scalar", lxs, t_ld)
            if tt >= 2:
                pl.wait("scalar", "sv", ln_done[tt - 2])
            pl.op("scalar", lambda e, s, xb=xb, tmp2s=tmp2s: e.activation(
                out=sqs[:, :], in_=xb, func=AF.Square, accum_out=tmp2s), inc="ss")
            s_sq = pl.val("ss")
            pl.wait("vector", "ss", s_sq)
            pl.op("vector", lambda e, s, tmp1s=tmp1s: e.tensor_scalar(
                out=tmp1s, in0=tmp1s, scalar1=1.0 / D, scalar2=None,
                op0=ALU.mult))
            pl.op("vector", lambda e, s, tmp2s=tmp2s: e.tensor_scalar(
                out=tmp2s, in0=tmp2s, scalar1=1.0 / D, scalar2=None,
                op0=ALU.mult))
            pl.op("vector", lambda e, s, tmp1s=tmp1s, tmp3s=tmp3s:
                  e.scalar_tensor_tensor(
                      out=tmp3s, in0=tmp1s, scalar=-1.0, in1=tmp1s,
                      op0=ALU.mult, op1=ALU.mult))
            pl.op("vector", lambda e, s, tmp2s=tmp2s, tmp3s=tmp3s: e.tensor_scalar(
                out=tmp2s, in0=tmp2s, scalar1=tmp3s, scalar2=1e-5,
                op0=ALU.add, op1=ALU.add), inc="sv")
            v_var = pl.val("sv")
            pl.wait("scalar", "sv", v_var)
            pl.op("scalar", lambda e, s, tmp2s=tmp2s: e.activation(
                out=tmp2s, in_=tmp2s, func=AF.Sqrt), inc="ss")
            pl.wait("vector", "ss", pl.val("ss"))
            pl.op("vector", lambda e, s, tmp2s=tmp2s: e.reciprocal(tmp2s, tmp2s))
            pl.op("vector", lambda e, s, tmp1s=tmp1s, tmp2s=tmp2s, tmp3s=tmp3s:
                  e.scalar_tensor_tensor(
                      out=tmp3s, in0=tmp1s, scalar=-1.0, in1=tmp2s,
                      op0=ALU.mult, op1=ALU.mult), inc="sv")
            v_nb = pl.val("sv")
            if tt >= 2:
                pl.wait("vector", "st", tr_last[tt - 2])
            pl.op("vector", lambda e, s, xb=xb, xs=xs, tmp2s=tmp2s, tmp3s=tmp3s:
                  e.tensor_scalar(
                      out=xs, in0=xb, scalar1=tmp2s, scalar2=tmp3s,
                      op0=ALU.mult, op1=ALU.add), inc="sv")
            ln_done[tt] = pl.val("sv")
            pl.wait("tensor", "sv", ln_done[tt])
            pl.wait("tensor", "sg", 2)
            for fp in range(8):
                fc = 2 * fp
                if len(tr_evac) >= 2:
                    pl.wait("tensor", "ss", tr_evac[-2])
                pb = pR0b if fp % 2 == 0 else pR1b
                pl.op("tensor", lambda e, s, xs=xs, fc=fc, pb=pb: e.transpose(
                    pb[:, 0:128],
                    xs[:, fc * 128:(fc + 1) * 128], ident[:, :]))
                pl.op("tensor", lambda e, s, xs=xs, fc=fc, pb=pb: e.transpose(
                    pb[:, 128:256],
                    xs[:, (fc + 1) * 128:(fc + 2) * 128], ident[:, :]), inc="st")
                t_tr = pl.val("st")
                pl.wait("scalar", "st", t_tr)
                pl.op("scalar", lambda e, s, fc=fc, tt=tt, pb=pb: e.activation(
                    out=xnT[:, :].rearrange("p (a b) -> p a b", b=T)
                    [:, fc:fc + 2, tt * 128:(tt + 1) * 128],
                    in_=pb[:, 0:256].rearrange("p (a b) -> p a b", b=128),
                    func=AF.Copy), inc="ss")
                tr_evac.append(pl.val("ss"))
            tr_last[tt] = pl.val("st")
        xn_ready = pl.val("ss")
        dbg_dxn_tok = 0
        if DEBUG:
            pl.wait("sync", "ss", xn_ready)
            dbg_dxn_tok = pl.dma("sync", dxn, xnT[:, :], "dbg")

        w_cons = [0, 0]

        dq_ev = []
        dq_mul = [0, 0]

        def dequant_load(codes, scales, c0, bsl):
            cb = wc[:, bsl * 4096:(bsl + 1) * 4096]
            se = sexp[:, bsl * 4096:(bsl + 1) * 4096]
            sr = wAr[0:4, 8192 + bsl * 4096: 8192 + (bsl + 1) * 4096]
            lw = "lw0" if bsl == 0 else "lw1"
            sr_sem = "srA" if bsl == 0 else "srB"
            if w_cons[bsl]:
                pl.wait("gpsimd", "st", w_cons[bsl])
            pl.dma("gpsimd", sr.rearrange("p (a c) -> p a c", c=256),
                   scales[:, c0:c0 + 256].rearrange("(a r) c -> r a c", r=4), sr_sem)
            t_s = pl.val(sr_sem)
            pl.dma("gpsimd", cb.rearrange("p (a c) -> p a c", c=256),
                   codes[:, c0:c0 + 256].rearrange("(a p) c -> p a c", p=128), lw)
            t_w = pl.val(lw)
            pl.wait("tensor", sr_sem, t_s)
            if dq_mul[bsl]:
                pl.wait("scalar", "sv", dq_mul[bsl])
            for wv in range(8):
                pb = pP2[:, (wv % 2) * 512:(wv % 2 + 1) * 512]
                if len(dq_ev) >= 2:
                    pl.wait("tensor", "ss", dq_ev[-2])
                pl.op("tensor", lambda e, s, wv=wv, pb=pb: e.matmul(
                    pb, exw[:, :], sr[:, wv * 512:(wv + 1) * 512],
                    start=True, stop=True), inc="st")
                t_mm = pl.val("st")
                pl.wait("scalar", "st", t_mm)
                pl.op("scalar", lambda e, s, wv=wv, pb=pb, se=se: e.activation(
                    out=se[:, wv * 512:(wv + 1) * 512], in_=pb,
                    func=AF.Copy), inc="ss")
                dq_ev.append(pl.val("ss"))
            pl.wait("vector", lw, t_w)
            pl.wait("vector", "ss", dq_ev[-1])
            pl.op("vector", lambda e, s, cb=cb, se=se:
                  e.tensor_mul(cb, cb, se), inc="sv")
            dq_mul[bsl] = pl.val("sv")
            return cb, pl.val("sv")

        def qkv_mms(wb, ps, j):
            for hc in range(16):
                for half in range(2):
                    pl.op("tensor", lambda e, s, hc=hc, half=half, wb=wb, ps=ps, j=j:
                          e.matmul(ps[:, half * 512:(half + 1) * 512],
                                   wb[:, hc * 256 + j * 128: hc * 256 + (j + 1) * 128],
                                   xnT[:, hc * T + half * 512: hc * T + (half + 1) * 512],
                                   start=(hc == 0), stop=(hc == 15)),
                          inc="st" if (hc == 15 and half == 1) else None)
            return pl.val("st")

        nrm_v_prev = [0]

        def norm_mms(dst_row, h, s_ready, ksq_s):
            pl.wait("tensor", "ss", s_ready)
            if nrm_v_prev[0]:
                pl.wait("tensor", "sv", nrm_v_prev[0])
            for half in range(2):
                pl.op("tensor", lambda e, s, half=half, ksq_s=ksq_s: e.matmul(
                    (pR0 if half == 0 else pR1)[0:1, :], ones_b[:, :],
                    ksq_s[:, half * 512:(half + 1) * 512], start=True, stop=True),
                    inc="st" if half == 1 else None)
            t_n = pl.val("st")
            pl.wait("vector", "st", t_n)
            pl.op("vector", lambda e, s: e.reduce_max(
                nrm_a[0:1, 0:1], pR0[0:1, :], axis=AXX))
            pl.op("vector", lambda e, s: e.reduce_max(
                nrm_a[0:1, 1:2], pR1[0:1, :], axis=AXX))
            pl.op("vector", lambda e, s, h=h: e.reduce_max(
                dst_row[0:1, h:h + 1], nrm_a[0:1, :], axis=AXX),
                inc="sv")
            nrm_v_prev[0] = pl.val("sv")
            return t_n, pl.val("sv")

        norm_t_prev = 0
        pend_norm = None
        evac_tok = {}
        store_tok = {}
        for h in range(H):
            bsl = h % 2
            if h % 2 == 0:
                dq_pair = dequant_load(ck, sk, h * 128, (h // 2) % 2)
            wb, v_w = dq_pair
            pl.wait("tensor", "sv", v_w)
            pl.wait("tensor", "ss", xn_ready)
            if h >= 2:
                pl.wait("tensor", "sv", evac_tok[h - 2])
            ps = pP0 if h % 2 == 0 else pP1
            t_mm = qkv_mms(wb, ps, h % 2)
            if h % 2 == 1:
                w_cons[(h // 2) % 2] = t_mm
            kb = kstg[:, bsl * T:(bsl + 1) * T]
            if h >= 2:
                pl.wait("vector", "sk0" if bsl == 0 else "sk1",
                        store_tok[h - 2])
            pl.wait("vector", "st", t_mm)
            pl.op("vector", lambda e, s, ps=ps, kb=kb, h=h: e.tensor_scalar(
                out=kb, in0=ps[:, :], scalar1=bkc[:, h:h + 1], scalar2=None,
                op0=ALU.add), inc="sv")
            s_k = pl.val("sv")
            evac_tok[h] = s_k
            if pend_norm is not None:
                norm_t_prev, _ = norm_mms(kmx, pend_norm[0], pend_norm[1],
                                          pend_norm[2])
            if norm_t_prev:
                pl.wait("scalar", "st", norm_t_prev)
            pl.wait("scalar", "sv", s_k)
            ksq_s = ksq0 if h % 2 == 0 else ksq1
            pl.op("scalar", lambda e, s, kb=kb, ksq_s=ksq_s: e.activation(
                out=ksq_s[:, :], in_=kb, func=AF.Square), inc="ss")
            pend_norm = (h, pl.val("ss"), ksq_s)
            pl.wait("sync", "sv", s_k)
            store_tok[h] = pl.dma("sync", kvk_in[h * 128:(h + 1) * 128, :], kb,
                                  "sk0" if bsl == 0 else "sk1")
        norm_t_prev, _ = norm_mms(kmx, pend_norm[0], pend_norm[1], pend_norm[2])
        pend_norm = None
        k_stored = (pl.val("sk0"), pl.val("sk1"))
        kn_done = pl.val("sv")

        vt_evac = []
        prev_evac = [evac_tok[H - 2], evac_tok[H - 1]]
        evac_tok = {}
        store_tok = {}
        for h in range(H):
            bsl = h % 2
            if h % 2 == 0:
                dq_pair = dequant_load(cv, svs, h * 128, (h // 2) % 2)
            wb, v_w = dq_pair
            pl.wait("tensor", "sv", v_w)
            pl.wait("tensor", "sv", evac_tok[h - 2] if h >= 2 else prev_evac[h])
            ps = pP0 if h % 2 == 0 else pP1
            t_mm = qkv_mms(wb, ps, h % 2)
            if h % 2 == 1:
                w_cons[(h // 2) % 2] = t_mm
            vb = vstg[:, bsl * T:(bsl + 1) * T]
            vt = vts[:, bsl * T:(bsl + 1) * T]
            if h >= 2:
                pl.wait("vector", "sk0" if bsl == 0 else "sk1", store_tok[h - 2])
            pl.wait("vector", "st", t_mm)
            pl.op("vector", lambda e, s, ps=ps, vb=vb, h=h: e.tensor_scalar(
                out=vb, in0=ps[:, :], scalar1=bvc[:, h:h + 1], scalar2=None,
                op0=ALU.add), inc="sv")
            s_v = pl.val("sv")
            evac_tok[h] = s_v
            pl.wait("tensor", "sv", s_v)
            for jp in range(4):
                j = 2 * jp
                if len(vt_evac) >= 2:
                    pl.wait("tensor", "sv", vt_evac[-2])
                pb = pR0b if jp % 2 == 0 else pR1b
                pl.op("tensor", lambda e, s, vb=vb, j=j, pb=pb: e.transpose(
                    pb[:, 0:128],
                    vb[:, j * 128:(j + 1) * 128], ident[:, :]))
                pl.op("tensor", lambda e, s, vb=vb, j=j, pb=pb: e.transpose(
                    pb[:, 128:256],
                    vb[:, (j + 1) * 128:(j + 2) * 128], ident[:, :]), inc="st")
                t_tr = pl.val("st")
                pl.wait("vector", "st", t_tr)
                pl.op("vector", lambda e, s, vt=vt, j=j, pb=pb: e.tensor_copy(
                    out=vt[:, j * 128:(j + 2) * 128],
                    in_=pb[:, 0:256]), inc="sv")
                vt_evac.append(pl.val("sv"))
            pl.wait("sync", "sv", pl.val("sv"))
            store_tok[h] = pl.dma(
                "sync",
                kvv_in[h * T:(h + 1) * T, :].rearrange("(j p) d -> p j d", p=128),
                vt.rearrange("p (j d) -> p j d", d=DH),
                "sk0" if bsl == 0 else "sk1")
        prev_evac = [evac_tok[H - 2], evac_tok[H - 1]]
        evac_tok = {}
        for h in range(H):
            bsl = h % 2
            if h % 2 == 0:
                dq_pair = dequant_load(cq, sq, h * 128, (h // 2) % 2)
            wb, v_w = dq_pair
            pl.wait("tensor", "sv", v_w)
            pl.wait("tensor", "sv", evac_tok[h - 2] if h >= 2 else prev_evac[h])
            ps = pP0 if h % 2 == 0 else pP1
            t_mm = qkv_mms(wb, ps, h % 2)
            if h % 2 == 1:
                w_cons[(h // 2) % 2] = t_mm
            pl.wait("vector", "st", t_mm)
            pl.op("vector", lambda e, s, ps=ps, h=h: e.tensor_scalar(
                out=QT[:, h * T:(h + 1) * T], in0=ps[:, :], scalar1=bqc[:, h:h + 1],
                scalar2=None, op0=ALU.add), inc="sv")
            s_q = pl.val("sv")
            evac_tok[h] = s_q
            if pend_norm is not None:
                norm_t_prev, _ = norm_mms(mq2, pend_norm[0], pend_norm[1],
                                          pend_norm[2])
            if norm_t_prev:
                pl.wait("scalar", "st", norm_t_prev)
            pl.wait("scalar", "sv", s_q)
            ksq_s = ksq0 if h % 2 == 0 else ksq1
            pl.op("scalar", lambda e, s, h=h, ksq_s=ksq_s: e.activation(
                out=ksq_s[:, :], in_=QT[:, h * T:(h + 1) * T], func=AF.Square),
                inc="ss")
            pend_norm = (h, pl.val("ss"), ksq_s)
        norm_t_prev, _ = norm_mms(mq2, pend_norm[0], pend_norm[1], pend_norm[2])
        qn_done = pl.val("sv")
        pl.wait("sync", "sv", kn_done)
        pl.dma("sync", kvn_in.ap().rearrange("a b -> b a"), kmx[0:1, :], "skn", slow=True)
        pl.wait("gpsimd", "sk0", pl.val("sk0"))
        pl.wait("gpsimd", "sk1", pl.val("sk1"))
        pl.wait("gpsimd", "skn", pl.val("skn"))
        for (i_t, o_t) in ((kvk_in, kvk_out), (kvv_in, kvv_out), (kvn_in, kvn_out)):
            pl.op("gpsimd", lambda e, s, i_t=i_t, o_t=o_t: e.collective_compute(
                "AllGather", ALU.bypass,
                replica_groups=[[0, 1, 2, 3], [4, 5, 6, 7]],
                ins=[i_t.ap().opt()], outs=[o_t.ap().opt()]), inc="cc")
        cc_done = pl.val("cc")
        q_done_s = pl.val("ss")
        qkv_t_last = pl.val("st")
        qkv_evac_last = [evac_tok[H - 2], evac_tok[H - 1]]
        if DEBUG:
            pl.wait("sync", "sv", qn_done)
            pl.dma("sync", dqt, QT[:, :], "dbg")

        gu_cons = [0, 0]
        dw_cons = {}
        hT_done = {}
        silu_tok = {}
        uev_tok = {}
        pl.wait("scalar", "sk0", pl.val("sk0"))
        pl.wait("scalar", "sk1", pl.val("sk1"))
        pl.wait("gpsimd", "st", qkv_t_last)
        pl.wait("tensor", "ss", pl.val("ss"))
        pl.wait("vector", "sk0", pl.val("sk0"))
        pl.wait("vector", "sk1", pl.val("sk1"))
        pair_w = {}
        down_done = {}

        def hts(blkd, ii):
            if ii >= 2:
                return hT[:, ii * T:(ii + 1) * T]
            base = hT if blkd % 2 == 0 else hTx
            return base[:, ii * T:(ii + 1) * T]

        rotd = [pP2[:, 0:512], pP2[:, 512:1024], pR0[:, :], pR1[:, :]]
        rot_tok = [0, 0, 0, 0]
        kdn = [0]

        def emit_down(blkd, lo, hi):
            pl.wait("tensor", "sv", hT_done[blkd * ITB + ITB - 1])
            for idx in range(lo, hi):
                tt, hc = idx // 4, idx % 4
                k = kdn[0]
                ps = rotd[k % 4]
                if rot_tok[k % 4]:
                    pl.wait("tensor", "sv", rot_tok[k % 4])
                for ii in range(ITB):
                    it = blkd * ITB + ii
                    dsl2 = (it // 2) % 3
                    jj2 = it % 2
                    pl.op("tensor",
                          lambda e, s, ps=ps, it=it, ii=ii, tt=tt, hc=hc,
                          dsl2=dsl2, jj2=jj2, blkd=blkd:
                          e.matmul(ps,
                                   hts(blkd, ii)[:, tt * 128:(tt + 1) * 128],
                                   dwb[:, dsl2 * 4096 + jj2 * 2048 + hc * 512:
                                       dsl2 * 4096 + jj2 * 2048 + (hc + 1) * 512],
                                   start=(ii == 0), stop=(ii == ITB - 1)),
                          inc="st" if ii == ITB - 1 else None)
                t_dn = pl.val("st")
                pl.wait("vector", "st", t_dn)
                fsl = facc[:, tt * D + hc * 512: tt * D + (hc + 1) * 512]
                if blkd == 0:
                    pl.op("vector", lambda e, s, fsl=fsl, ps=ps:
                          e.tensor_copy(out=fsl, in_=ps), inc="sv")
                else:
                    pl.op("vector", lambda e, s, fsl=fsl, ps=ps:
                          e.tensor_add(fsl, fsl, ps), inc="sv")
                rot_tok[k % 4] = pl.val("sv")
                kdn[0] += 1
            if hi == 32:
                tok = pl.val("st")
                for ii in range(ITB):
                    dw_cons[blkd * ITB + ii] = tok
                down_done[blkd] = tok

        for blk in range(NITB):
            for ii in range(ITB):
                it = blk * ITB + ii
                if it % 2 == 0:
                    pi = it // 2
                    psl_w = pi % 2
                    gwb2 = gw[:, psl_w * 4096:(psl_w + 1) * 4096]
                    uwb2 = uw[:, psl_w * 4096:(psl_w + 1) * 4096]
                    if gu_cons[psl_w]:
                        pl.wait("gpsimd", "st", gu_cons[psl_w])
                    lw = "lw0" if psl_w == 0 else "lw1"
                    pl.dma("gpsimd", gwb2.rearrange("p (a c) -> p a c", c=256),
                           wgT[:, it * 128:(it + 2) * 128]
                           .rearrange("(a p) c -> p a c", p=128), lw)
                    pl.dma("gpsimd", uwb2.rearrange("p (a c) -> p a c", c=256),
                           wuT[:, it * 128:(it + 2) * 128]
                           .rearrange("(a p) c -> p a c", p=128), lw)
                    dsl2 = pi % 3
                    dwv2 = dwb[:, dsl2 * 4096:(dsl2 + 1) * 4096]
                    if pi >= 3:
                        pl.wait("gpsimd", "st", dw_cons[(pi - 3) * 2])
                    pl.dma("gpsimd", dwv2.rearrange("p (a c) -> p a c", c=2048),
                           wdT[it * 128:(it + 2) * 128, :]
                           .rearrange("(a p) c -> p a c", p=128), lw)
                    t_w = pl.val(lw)
                    pl.wait("vector", lw, t_w)
                    pl.op("vector", lambda e, s, gwb2=gwb2: e.tensor_scalar(
                        out=gwb2, in0=gwb2, scalar1=128.0, scalar2=None,
                        op0=ALU.subtract))
                    pl.op("vector", lambda e, s, uwb2=uwb2: e.tensor_scalar(
                        out=uwb2, in0=uwb2, scalar1=128.0, scalar2=None,
                        op0=ALU.subtract))
                    pl.op("vector", lambda e, s, dwv2=dwv2: e.tensor_scalar(
                        out=dwv2, in0=dwv2, scalar1=128.0, scalar2=None,
                        op0=ALU.subtract), inc="sv")
                    pair_w[pi] = (pl.val("sv"), gwb2, uwb2, dwv2)
                pi = it // 2
                jj = it % 2
                v_w, gwb2, uwb2, dwv2 = pair_w[pi]
                if blk >= 1 and ii < 2:
                    emit_down(blk - 1, ii * 16, (ii + 1) * 16)
                pl.wait("tensor", "sv", v_w)
                if it >= 1:
                    pl.wait("tensor", "ss", silu_tok[it - 1])
                    pl.wait("tensor", "sv", uev_tok[it - 1])
                else:
                    pl.wait("tensor", "sv", qkv_evac_last[1])
                for (wbv, ps) in ((gwb2, pP0), (uwb2, pP1)):
                    for hc in range(16):
                        for half in range(2):
                            pl.op("tensor",
                                  lambda e, s, wbv=wbv, ps=ps, hc=hc, half=half, jj=jj:
                                  e.matmul(ps[:, half * 512:(half + 1) * 512],
                                           wbv[:, hc * 256 + jj * 128: hc * 256 + (jj + 1) * 128],
                                           xnT[:, hc * T + half * 512: hc * T + (half + 1) * 512],
                                           start=(hc == 0), stop=(hc == 15)),
                                  inc="st" if (hc == 15 and half == 1) else None)
                if jj == 1:
                    gu_cons[pi % 2] = pl.val("st")
                t_gu = pl.val("st")
                gtile = kstg[:, 0:T]
                utile = kstg[:, T:2 * T]
                pl.wait("scalar", "st", t_gu)
                if it >= 1:
                    pl.wait("scalar", "sv", hT_done[it - 1])
                pl.op("scalar", lambda e, s, it=it, gtile=gtile: e.activation(
                    out=gtile, in_=pP0[:, :], func=AF.Silu,
                    scale=gscol[:, it:it + 1]), inc="ss")
                s_g = pl.val("ss")
                silu_tok[it] = s_g
                pl.wait("vector", "st", t_gu)
                pl.op("vector", lambda e, s, it=it, utile=utile: e.tensor_scalar(
                    out=utile, in0=pP1[:, :], scalar1=uscol[:, it:it + 1],
                    scalar2=None, op0=ALU.mult), inc="sv")
                uev_tok[it] = pl.val("sv")
                pl.wait("vector", "ss", s_g)
                if ii < 2 and blk >= 2:
                    pl.wait("vector", "st", down_done[blk - 2])
                pl.op("vector", lambda e, s, ii=ii, blk=blk, gtile=gtile, utile=utile:
                      e.tensor_mul(hts(blk, ii), gtile, utile),
                      inc="sv")
                hT_done[it] = pl.val("sv")
        emit_down(NITB - 1, 0, 32)
        ffn_t_last = pl.val("st")

        pl.wait("vector", "sv", qn_done)
        pl.wait("sync", "cc", cc_done)
        pl.dma("sync", knr[0:1, :], kvn_out[:, :].rearrange("a b -> b a"), "knl", slow=True)
        t_knr = pl.val("knl")
        pl.wait("vector", "knl", t_knr)
        pl.op("vector", lambda e, s: e.reduce_max(
            mk2[0:1, :].rearrange("a (h o) -> a h o", o=1),
            knr[0:1, :].rearrange("a (sh h) -> a h sh", h=H), axis=AXX), inc="sv")
        v_rows = pl.val("sv")
        pl.wait("tensor", "sv", v_rows)
        pl.op("tensor", lambda e, s: e.matmul(
            pR0[:, 0:H], ones_f[:, :], mq2[:, :], start=True, stop=True))
        pl.op("tensor", lambda e, s: e.matmul(
            pR1[:, 0:H], ones_f[:, :], mk2[:, :], start=True, stop=True), inc="st")
        t_rows = pl.val("st")
        pl.wait("vector", "st", t_rows)
        pl.op("vector", lambda e, s: e.tensor_copy(out=csb[:, :], in_=pR1[:, 0:H]))
        pl.op("vector", lambda e, s: e.tensor_mul(csb[:, :], pR0[:, 0:H], csb[:, :]),
              inc="sv")
        pl.wait("scalar", "sv", pl.val("sv"))
        pl.op("scalar", lambda e, s: e.activation(
            out=csb[:, :], in_=csb[:, :], func=AF.Sqrt), inc="ss")
        pl.wait("vector", "ss", pl.val("ss"))
        pl.op("vector", lambda e, s: e.tensor_scalar(
            out=csb[:, :], in0=csb[:, :], scalar1=-RSQ, scalar2=None,
            op0=ALU.mult), inc="sv")
        csb_done = pl.val("sv")
        pl.wait("tensor", "sv", pl.val("sv"))
        dsb_cp = {}
        for j in range(4):
            ps = pR0 if j % 2 == 0 else pR1
            pl.dma("sync", dsrow[:, :],
                   dsc[j * 512:(j + 1) * 512].rearrange("(o a) -> o a", o=1), "ldc")
            t_dsr = pl.val("ldc")
            pl.wait("tensor", "ldc", t_dsr)
            if j >= 2:
                pl.wait("tensor", "sv", dsb_cp[j - 2])
            pl.op("tensor", lambda e, s, j=j, ps=ps: e.matmul(
                ps[:, :], ones_f[:, :], dsrow[:, :],
                start=True, stop=True), inc="st")
            tv = pl.val("st")
            pl.wait("vector", "st", tv)
            dsb_cp[j] = pl.op("vector", lambda e, s, j=j, ps=ps: e.tensor_copy(
                out=dsb[:, j * 512:(j + 1) * 512], in_=ps[:, :]), inc="sv")
            if j < 3:
                pl.wait("sync", "st", tv)
        v_dsb = pl.val("sv")
        fx_done = {}
        pl.wait("vector", "sv", v_dsb)
        pl.wait("sync", "st", ffn_t_last)
        for tt in range(NT):
            xb = xbuf[:, 0:D]
            if tt >= 1:
                pl.wait("sync", "sv", fx_done[tt - 1])
            pl.dma("sync", xb, x_in[tt * 128:(tt + 1) * 128, :], "ldx")
            t_ld = pl.val("ldx")
            pl.wait("vector", "ldx", t_ld)
            fsl = facc[:, tt * D:(tt + 1) * D]
            pl.op("vector", lambda e, s, fsl=fsl: e.tensor_mul(fsl, fsl, dsb[:, :]))
            pl.op("vector", lambda e, s, fsl=fsl, xb=xb: e.tensor_add(fsl, fsl, xb),
                  inc="sv")
            fx_done[tt] = pl.val("sv")
        facc_done = pl.val("sv")
        if DEBUG:
            pl.wait("sync", "sv", csb_done)
            pl.dma("sync", dnr[0:1, 0:H], mq2[0:1, :], "dbg")
            pl.dma("sync", dnr[0:1, H:2 * H], mk2[0:1, :], "dbg")
            pl.dma("sync", dnr[0:1, 2 * H:3 * H], csb[0:1, :], "dbg")
            pl.dma("sync", dnr[0:1, 3 * H:3 * H + SHD * H], knr[0:1, :], "dbg")
            pl.dma("sync", dcs, csb[:, :], "dbg")
            pl.dma("sync", dkk, kvk_out[:, :], "dbg")
            pl.dma("sync", dkv, kvv_out[:, :], "dbg")
            pl.dma("sync", dkn, kvn_out[:, :], "dbg")
        pl.wait("scalar", "sv", csb_done)
        pl.wait("tensor", "sv", facc_done if False else v_dsb)
        pl.wait("tensor", "sv", csb_done)

        kv_cons = {}
        exp_done = {}
        av_done = {}
        s_done = {}
        rbs_tok = {}
        norm_tok = {}

        def head_finish(h):
            emit_avl(h, 31)
            t_av = av_done[31]
            pl.wait("vector", "st", t_av)
            pl.op("vector", lambda e, s: e.reciprocal(rrow[0:1, 0:512], pR0[0:1, :]))
            pl.op("vector", lambda e, s: e.reciprocal(rrow[0:1, 512:1024], pR1[0:1, :]),
                  inc="sv")
            v_r = pl.val("sv")
            pl.wait("tensor", "sv", v_r)
            pl.op("tensor", lambda e, s: e.matmul(
                pR0[:, :], ones_f[:, :], rrow[0:1, 0:512], start=True, stop=True))
            pl.op("tensor", lambda e, s: e.matmul(
                pR1[:, :], ones_f[:, :], rrow[0:1, 512:1024], start=True, stop=True),
                inc="st")
            t_rb = pl.val("st")
            pl.wait("vector", "st", t_rb)
            pl.op("vector", lambda e, s: e.tensor_copy(out=rbs[:, 0:512], in_=pR0[:, :]))
            pl.op("vector", lambda e, s: e.tensor_copy(out=rbs[:, 512:1024], in_=pR1[:, :]),
                  inc="sv")
            rbs_tok[h] = pl.val("sv")
            pl.op("vector", lambda e, s, h=h: e.tensor_mul(
                aoT[:, h * T:(h + 1) * T], pP2[:, :], rbs[:, :]), inc="sv")
            norm_tok[h] = pl.val("sv")

        def emit_avl(h, gkc):
            vbp = vbf[:, (gkc // 8 % 2) * T:((gkc // 8 % 2) + 1) * T]
            kc = gkc % 8
            pslot = (gkc % 2) * T
            pl.wait("tensor", "ss", exp_done[gkc])
            if gkc == 0 and h >= 1:
                pl.wait("tensor", "sv", rbs_tok[h - 1])
                pl.wait("tensor", "sv", norm_tok[h - 1])
            for half in range(2):
                pl.op("tensor",
                      lambda e, s, vbp=vbp, kc=kc, half=half, pslot=pslot, gkc=gkc:
                      e.matmul(pP2[:, half * 512:(half + 1) * 512],
                               vbp[:, kc * 128:(kc + 1) * 128],
                               ptb[:, pslot + half * 512: pslot + (half + 1) * 512],
                               start=(gkc == 0), stop=(gkc == 31)))
                pl.op("tensor", lambda e, s, half=half, pslot=pslot, gkc=gkc:
                      e.matmul((pR0 if half == 0 else pR1)[0:1, :],
                               ones_b[:, :],
                               ptb[:, pslot + half * 512: pslot + (half + 1) * 512],
                               start=(gkc == 0), stop=(gkc == 31)),
                      inc="st" if half == 1 else None)
            av_done[gkc] = pl.val("st")
            if gkc % 8 == 7:
                kv_cons[(h, gkc // 8)] = av_done[gkc]
        if DEBUG:
            pl.wait("vector", "dbg", pl.val("dbg"))
        pl.wait("sync", "st", ffn_t_last)
        for h in range(H):
            for sh in range(SHD):
                bsl = sh % 2
                kb = kbuf[:, bsl * T:(bsl + 1) * T]
                vb = vbf[:, bsl * T:(bsl + 1) * T]
                prev = (h, sh - 2) if sh >= 2 else ((h - 1, sh + 2) if h >= 1 else None)
                if prev in kv_cons:
                    pl.wait("sync", "st", kv_cons[prev])
                kvs = "kv0" if bsl == 0 else "kv1"
                pl.dma("sync", kb,
                       kvk_out[(sh * H + h) * DH:(sh * H + h + 1) * DH, :], kvs)
                pl.dma("sync", vb.rearrange("p (j d) -> p j d", d=DH),
                       kvv_out[(sh * H + h) * T:(sh * H + h + 1) * T, :]
                       .rearrange("(j p) d -> p j d", p=128), kvs)
                t_kv = pl.val(kvs)
                pl.wait("tensor", kvs, t_kv)
                for kc in range(8):
                    gkc = sh * 8 + kc
                    psl = pP0 if gkc % 2 == 0 else pP1
                    pslot = (gkc % 2) * T
                    if gkc >= 2:
                        pl.wait("tensor", "ss", exp_done[gkc - 2])
                    elif h >= 1 and gkc == 0:
                        pl.wait("tensor", "ss", exp_done[30])
                    elif h >= 1 and gkc == 1:
                        pl.wait("tensor", "ss", exp_done[31])
                    for half in range(2):
                        pl.op("tensor",
                              lambda e, s, kb=kb, kc=kc, h=h, half=half, psl=psl:
                              e.matmul(psl[:, half * 512:(half + 1) * 512],
                                       kb[:, kc * 128:(kc + 1) * 128],
                                       QT[:, h * T + half * 512: h * T + (half + 1) * 512],
                                       start=True, stop=True),
                              inc="st" if half == 1 else None)
                    s_done[gkc] = pl.val("st")
                    pl.wait("scalar", "st", s_done[gkc])
                    if gkc >= 2:
                        pl.wait("scalar", "st", av_done[gkc - 2])
                    pl.op("scalar", lambda e, s, psl=psl, pslot=pslot, h=h:
                          e.activation(out=ptb[:, pslot:pslot + T], in_=psl[:, :],
                                       func=AF.Exp, bias=csb[:, h:h + 1],
                                       scale=RSQ), inc="ss")
                    exp_done[gkc] = pl.val("ss")
                    if gkc >= 1:
                        emit_avl(h, gkc - 1)
                    elif h >= 1:
                        head_finish(h - 1)
        head_finish(H - 1)
        attn_done = pl.val("sv")
        if DEBUG:
            pl.wait("sync", "sv", attn_done)
            pl.dma("sync", dao, aoT[:, :], "dbg")
            pl.wait("sync", "sv", facc_done)
            pl.dma("sync", dfa, facc[:, :], "dbg")

        pl.wait("tensor", "sv", attn_done)
        pl.wait("tensor", "sv", facc_done)
        pl.wait("sync", "sv", attn_done)
        wo_cons = [0, 0]
        wo_tt = 0
        ost_tok = {}
        attn_t_last = pl.val("st")
        pl.wait("gpsimd", "st", attn_t_last)
        pl.wait("gpsimd", "sv", facc_done)
        woscl = dAr[:, 0:8192]
        rot = [pP2[:, 0:512], pP2[:, 512:1024], pP0[:, 0:512], pP0[:, 512:1024]]
        rot_tok = [0, 0, 0, 0]
        k = 0
        bo_mm_tok = 0
        for dc in range(4):
            if bo_mm_tok:
                pl.wait("sync", "st", bo_mm_tok)
            pl.dma("sync", bor[:, :],
                   bo[dc * 512:(dc + 1) * 512].rearrange("(o a) -> o a", o=1), "ldc")
            t_bor = pl.val("ldc")
            pl.wait("tensor", "ldc", t_bor)
            wsl = dc % 2
            w0 = woc[:, wsl * 8192:(wsl + 1) * 8192]
            lws = "lw0" if wsl == 0 else "lw1"
            if wo_cons[wsl]:
                pl.wait("gpsimd", "st", wo_cons[wsl])
            pl.dma("gpsimd", w0.rearrange("p (a u c) -> p a u c", u=4, c=128),
                   co[:, dc * 512:(dc + 1) * 512]
                   .rearrange("(a p) (u c) -> p a u c", p=128, c=128), lws)
            if wo_tt:
                pl.wait("gpsimd", "sv", wo_tt)
            pl.dma("gpsimd", woscl.rearrange("p (a u c) -> p a u c", u=4, c=128),
                   so[:, dc * 512:(dc + 1) * 512]
                   .rearrange("(a p) (u c) -> p a u c", p=128, c=128), lws)
            t_wo = pl.val(lws)
            pl.wait("vector", lws, t_wo)
            pl.op("vector", lambda e, s, w0=w0: e.tensor_mul(w0, w0, woscl),
                  inc="sv")
            v_wo = pl.val("sv")
            wo_tt = v_wo
            pl.wait("tensor", "sv", v_wo)
            for tt in range(NT):
                ps = rot[k % 4]
                if rot_tok[k % 4]:
                    pl.wait("tensor", "sv", rot_tok[k % 4])
                for fc in range(16):
                    pl.op("tensor", lambda e, s, ps=ps, fc=fc, tt=tt, w0=w0:
                          e.matmul(ps,
                                   aoT[:, fc * T + tt * 128: fc * T + (tt + 1) * 128],
                                   w0[:, fc * 512:(fc + 1) * 512],
                                   start=(fc == 0), stop=False))
                pl.op("tensor", lambda e, s, ps=ps: e.matmul(
                    ps, ones_f[:, :], bor[0:1, :],
                    start=False, stop=True), inc="st")
                t_op = pl.val("st")
                bo_mm_tok = t_op
                osl = outs[:, (k % 2) * 512:(k % 2 + 1) * 512]
                if k >= 2:
                    pl.wait("vector", "st0" if k % 2 == 0 else "st1",
                            ost_tok[k - 2])
                pl.wait("vector", "st", t_op)
                pl.op("vector", lambda e, s, ps=ps, osl=osl, tt=tt, dc=dc:
                      e.tensor_add(osl, ps,
                                   facc[:, tt * D + dc * 512: tt * D + (dc + 1) * 512]),
                      inc="sv")
                rot_tok[k % 4] = pl.val("sv")
                pl.wait("sync", "sv", rot_tok[k % 4])
                ost_tok[k] = pl.dma(
                    "sync",
                    y_out[tt * 128:(tt + 1) * 128, dc * 512:(dc + 1) * 512],
                    osl, "st0" if k % 2 == 0 else "st1")
                k += 1
            wo_cons[wsl] = pl.val("st")
        pl.wait("sync", "st0", pl.val("st0"))
        pl.wait("sync", "st1", pl.val("st1"))

        block = ctx.enter_context(nc.Block())
        S_ = sems

        @block.sync
        def _(e):
            for fn in pl.q["sync"]:
                fn(e, S_)

        @block.gpsimd
        def _(e):
            for fn in pl.q["gpsimd"]:
                fn(e, S_)

        @block.vector
        def _(e):
            for fn in pl.q["vector"]:
                fn(e, S_)

        @block.scalar
        def _(e):
            for fn in pl.q["scalar"]:
                fn(e, S_)

        @block.tensor
        def _(e):
            for fn in pl.q["tensor"]:
                fn(e, S_)

    return nc


_NC_CACHE = None


def kernel(**inputs):
    global _NC_CACHE
    x = np.asarray(inputs["L0_attn_x"])
    x2d = np.ascontiguousarray(x.reshape(B * S, D))
    rep = lambda s: np.ascontiguousarray(np.repeat(np.asarray(s), 32, axis=0))
    base = {
        "cq": np.ascontiguousarray(np.asarray(inputs["L0_attn_wq_codes"])),
        "ck": np.ascontiguousarray(np.asarray(inputs["L0_attn_wk_codes"])),
        "cv": np.ascontiguousarray(np.asarray(inputs["L0_attn_wv_codes"])),
        "co": np.ascontiguousarray(np.asarray(inputs["L0_attn_wo_codes"])),
        "sq": np.ascontiguousarray(np.asarray(inputs["L0_attn_wq_scales"])),
        "sk": np.ascontiguousarray(np.asarray(inputs["L0_attn_wk_scales"])),
        "svs": np.ascontiguousarray(np.asarray(inputs["L0_attn_wv_scales"])),
        "so": rep(inputs["L0_attn_wo_scales"]),
        "bq": np.ascontiguousarray(np.asarray(inputs["L0_attn_bq"])),
        "bqcd": np.ascontiguousarray(np.asarray(inputs["L0_attn_bq"]).reshape(16, 128).T),
        "bkcd": np.ascontiguousarray(np.asarray(inputs["L0_attn_bk"]).reshape(16, 128).T),
        "bvcd": np.ascontiguousarray(np.asarray(inputs["L0_attn_bv"]).reshape(16, 128).T),
        "gscd": np.ascontiguousarray(np.asarray(inputs["L0_ffn_gate_scale"]).reshape(64, 128).T),
        "uscd": np.ascontiguousarray(np.asarray(inputs["L0_ffn_up_scale"]).reshape(64, 128).T),
        "bk": np.ascontiguousarray(np.asarray(inputs["L0_attn_bk"])),
        "bv": np.ascontiguousarray(np.asarray(inputs["L0_attn_bv"])),
        "bo": np.ascontiguousarray(np.asarray(inputs["L0_attn_bo"])),
        "wgT": np.ascontiguousarray(np.asarray(inputs["L0_ffn_gate_w"]).T),
        "wuT": np.ascontiguousarray(np.asarray(inputs["L0_ffn_up_w"]).T),
        "wdT": np.ascontiguousarray(np.asarray(inputs["L0_ffn_down_w"]).T),
        "gsc": np.ascontiguousarray(np.asarray(inputs["L0_ffn_gate_scale"])),
        "usc": np.ascontiguousarray(np.asarray(inputs["L0_ffn_up_scale"])),
        "dsc": np.ascontiguousarray(np.asarray(inputs["L0_ffn_down_scale"])),
    }
    in_maps = []
    for c in range(NCORES):
        m = dict(base)
        m["x"] = np.ascontiguousarray(x2d[c * T:(c + 1) * T, :])
        in_maps.append(m)
    if _NC_CACHE is None:
        _NC_CACHE = build_nc()
    res = run_bass_kernel_spmd(_NC_CACHE, in_maps, core_ids=list(range(NCORES)))
    out = np.concatenate([np.asarray(r["y"]) for r in res.results], axis=0)
    return out.reshape(B, S, D)


# version 18
# speedup vs baseline: 1.0214x; 1.0214x over previous
import numpy as np
import ml_dtypes
import concourse.bass as bass
import concourse.mybir as mybir
from concourse.bass_utils import run_bass_kernel_spmd

F32 = mybir.dt.float32
BF16 = mybir.dt.bfloat16
I32 = mybir.dt.int32
AF = mybir.ActivationFunctionType
ALU = mybir.AluOpType
AXX = mybir.AxisListType.X

DEBUG = False
NCORES = 8
B, S, D = 2, 4096, 2048
H, DH = 16, 128
FI = 8192
T = 1024
NT = T // 128
SHD = 4
RSQ = float(1.0 / np.sqrt(128.0))
ITB = 4
NITB = (FI // 128) // ITB


class Plan:

    def __init__(self):
        self.q = {k: [] for k in ("sync", "gpsimd", "vector", "scalar", "tensor")}
        self.cnt = {}

    def sem(self, name):
        self.cnt.setdefault(name, 0)

    def val(self, name):
        return self.cnt[name]

    def op(self, eng, fn, inc=None):
        if eng == "vector" or eng == "scalar":
            auto = "sv" if eng == "vector" else "ss"
            prev = self.cnt[auto]
            if prev > 0:
                self.q[eng].append(
                    lambda e, s, sem=auto, v=prev: e.wait_ge(s[sem], v))
            self.cnt[auto] += 1
            self.q[eng].append(
                lambda e, s, sem=auto: fn(e, s).then_inc(s[sem], 1))
            return self.cnt[auto]
        if inc is None:
            self.q[eng].append(lambda e, s: fn(e, s))
            return None
        self.cnt[inc] += 1
        self.q[eng].append(lambda e, s, inc=inc: fn(e, s).then_inc(s[inc], 1))
        return self.cnt[inc]

    def dma(self, eng, out, in_, sem, slow=False):
        self.cnt[sem] += 16
        self.q[eng].append(lambda e, s, out=out, in_=in_, sem=sem, slow=slow:
                           e.dma_start(out=out, in_=in_,
                                       allow_slow_non_contiguous=slow)
                           .then_inc(s[sem], 16))
        return self.cnt[sem]

    def wait(self, eng, sem, v=None):
        if v is None:
            v = self.cnt[sem]
        if v <= 0:
            return
        self.q[eng].append(lambda e, s, sem=sem, v=v: e.wait_ge(s[sem], v))


def build_nc():
    nc = bass.Bass("TRN2", target_bir_lowering=False, debug=False)

    din = lambda n, sh, dt: nc.dram_tensor(n, sh, dt, kind="ExternalInput").ap()
    x_in = din("x", [T, D], F32)
    cq, ck, cv, co = (din(n, [D, D], BF16) for n in ("cq", "ck", "cv", "co"))
    sq, sk, svs, so = (din(n, [D, D], BF16) for n in ("sq", "sk", "svs", "so"))
    bq, bk, bv, bo = (din(n, [D], F32) for n in ("bq", "bk", "bv", "bo"))
    bqcd = din("bqcd", [128, H], F32)
    bkcd = din("bkcd", [128, H], F32)
    bvcd = din("bvcd", [128, H], F32)
    gscd = din("gscd", [128, FI // 128], F32)
    uscd = din("uscd", [128, FI // 128], F32)
    wgT = din("wgT", [D, FI], BF16)
    wuT = din("wuT", [D, FI], BF16)
    wdT = din("wdT", [FI, D], BF16)
    gsc = din("gsc", [FI], F32)
    usc = din("usc", [FI], F32)
    dsc = din("dsc", [D], F32)
    y_out = nc.dram_tensor("y", [T, D], F32, kind="ExternalOutput").ap()
    if DEBUG:
        dxn = nc.dram_tensor("dxn", [128, H * T], BF16, kind="ExternalOutput").ap()
        dqt = nc.dram_tensor("dqt", [128, H * T], BF16, kind="ExternalOutput").ap()
        dkk = nc.dram_tensor("dkk", [SHD * H * DH, T], BF16, kind="ExternalOutput").ap()
        dkv = nc.dram_tensor("dkv", [SHD * H * T, DH], BF16, kind="ExternalOutput").ap()
        dkn = nc.dram_tensor("dkn", [SHD * H, 1], F32, kind="ExternalOutput").ap()
        dcs = nc.dram_tensor("dcs", [128, H], F32, kind="ExternalOutput").ap()
        dao = nc.dram_tensor("dao", [128, H * T], BF16, kind="ExternalOutput").ap()
        dfa = nc.dram_tensor("dfa", [128, NT * D], F32, kind="ExternalOutput").ap()
        dnr = nc.dram_tensor("dnr", [1, 3 * H + SHD * H], F32, kind="ExternalOutput").ap()

    kvk_in = nc.dram_tensor("kvk_in", [H * DH, T], BF16)
    kvv_in = nc.dram_tensor("kvv_in", [H * T, DH], BF16)
    kvn_in = nc.dram_tensor("kvn_in", [H, 1], F32)
    kvk_out = nc.dram_tensor("kvk_out", [SHD * H * DH, T], BF16)
    kvv_out = nc.dram_tensor("kvv_out", [SHD * H * T, DH], BF16)
    kvn_out = nc.dram_tensor("kvn_out", [SHD * H, 1], F32)

    from contextlib import ExitStack
    with ExitStack() as ctx:
        sb = lambda n, shp, dt: ctx.enter_context(nc.sbuf_tensor(n, shp, dt))
        xnT = sb("xnT", [128, H * T], BF16)
        aoT = xnT
        QT = sb("QT", [128, H * T], BF16)
        facc = sb("facc", [128, NT * D], F32)
        wAr = sb("wAr", [128, 16384], BF16)
        wc = wAr[:, 0:8192]
        ws = wAr[:, 8192:16384]
        gw = wAr[:, 0:8192]
        uw = wAr[:, 8192:16384]
        woc = wAr[:, 0:16384]
        dAr = sb("dAr", [128, (ITB + 2) * D], BF16)
        dwb = dAr[:, :]
        kbuf = dAr[:, 0:2048]
        vbf = dAr[:, 2048:4096]
        ptb = dAr[:, 4096:6144]
        xbuf = dAr[:, 6144:10240].bitcast(F32)
        aAr = sb("aAr", [128, 2 * D], BF16)
        xnstg = aAr[:, :]
        vstg = aAr[:, 0:2048]
        vts = aAr[:, 2048:4096]
        hT = aAr[:, :]
        ksq0 = aAr[:, 0:1024]
        ksq1 = aAr[:, 1024:2048]
        sqs = None
        kstg = sb("kstg", [128, 2 * T], BF16)
        rbs = kstg[:, :].bitcast(F32)[:, 0:1024]
        outs = aAr[:, :].bitcast(F32)[:, 0:1024]
        ident = sb("ident", [128, 128], BF16)
        ones_b = sb("ones_b", [128, 1], BF16)
        ones_f = sb("ones_f", [1, 128], F32)
        bqc = sb("bqc", [128, H], F32)
        bkc = sb("bkc", [128, H], F32)
        bvc = sb("bvc", [128, H], F32)
        gscol = sb("gscol", [128, FI // 128], F32)
        uscol = sb("uscol", [128, FI // 128], F32)
        hTx = sb("hTx", [128, 2048], BF16)
        rowA = sb("rowA", [1, T], F32)
        dsrow = rowA[0:1, 0:512]
        rrow = rowA[0:1, 0:T]
        bor = rowA[0:1, 0:512]
        dsb = wAr[:, 0:4096].bitcast(F32)
        csb = sb("csb", [128, H], F32)
        knr = sb("knr", [1, SHD * H], F32)
        knc = sb("knc", [SHD * H, 1], F32)
        mq2 = sb("mq2", [1, H], F32)
        kmx = sb("kmx", [1, H], F32)
        mk2 = sb("mk2", [1, H], F32)
        crow = sb("crow", [1, H], F32)
        nrm_a = sb("nrm_a", [1, 2], F32)
        tmp1 = sb("tmp1", [128, 2], F32)
        tmp2 = sb("tmp2", [128, 2], F32)
        tmp3 = sb("tmp3", [128, 2], F32)

        pP0 = ctx.enter_context(nc.psum_tensor("pP0", [128, 1024], F32))
        pP1 = ctx.enter_context(nc.psum_tensor("pP1", [128, 1024], F32))
        pP2 = ctx.enter_context(nc.psum_tensor("pP2", [128, 1024], F32))
        pR0 = ctx.enter_context(nc.psum_tensor("pR0", [128, 512], F32))
        pR1 = ctx.enter_context(nc.psum_tensor("pR1", [128, 512], F32))
        pR0b = pR0[:, :].bitcast(BF16)
        pR1b = pR1[:, :].bitcast(BF16)

        sem_names = ["ldx", "lx1", "ldc", "ldg", "lw0", "lw1", "kv0", "kv1",
                     "sk0", "sk1", "skn", "knl", "st0", "st1", "dbg", "cc", "sv", "ss", "st", "sg"]
        sems = {n: ctx.enter_context(nc.semaphore(n)) for n in sem_names}
        pl = Plan()
        for n in sem_names:
            pl.sem(n)
        sqs = facc[:, 0:D]

        pl.dma("gpsimd", bqc[:, :], bqcd, "ldg")
        pl.dma("gpsimd", bkc[:, :], bkcd, "ldg")
        pl.dma("gpsimd", bvc[:, :], bvcd, "ldg")
        pl.dma("gpsimd", gscol[:, :], gscd, "ldg")
        pl.dma("gpsimd", uscol[:, :], uscd, "ldg")
        tok_const = pl.val("ldg")
        pl.op("gpsimd", lambda e, s: e.memset(ident[:, :], 0.0), inc="sg")
        pl.wait("gpsimd", "sg", pl.val("sg"))
        pl.op("gpsimd", lambda e, s: e.affine_select(
            out=ident[:, :], in_=ident[:, :], compare_op=ALU.not_equal,
            fill=1.0, base=0, pattern=[[-1, 128]], channel_multiplier=1), inc="sg")
        pl.op("vector", lambda e, s: e.memset(ones_b[:, :], 1.0))
        pl.op("vector", lambda e, s: e.memset(ones_f[:, :], 1.0), inc="sv")
        v_ones = pl.val("sv")
        pl.wait("tensor", "sv", v_ones)
        pl.wait("tensor", "ldg", tok_const)
        pl.wait("vector", "ldg", tok_const)
        pl.wait("scalar", "ldg", tok_const)

        ln_done = {}
        tr_last = {}
        tr_evac = []
        for tt in range(NT):
            sl = tt % 2
            xb = xbuf[:, 0:D] if sl == 0 else facc[:, D:2 * D]
            xs = xnstg[:, sl * D:(sl + 1) * D]
            tmp1s = tmp1[:, sl:sl + 1]
            tmp2s = tmp2[:, sl:sl + 1]
            tmp3s = tmp3[:, sl:sl + 1]
            if tt >= 2:
                pl.wait("sync", "sv", ln_done[tt - 2])
            lxs = "ldx" if sl == 0 else "lx1"
            pl.dma("sync", xb, x_in[tt * 128:(tt + 1) * 128, :], lxs)
            t_ld = pl.val(lxs)
            pl.wait("vector", lxs, t_ld)
            pl.op("vector", lambda e, s, xb=xb, tmp1s=tmp1s: e.reduce_sum(
                tmp1s, xb, axis=AXX), inc="sv")
            pl.wait("scalar", lxs, t_ld)
            if tt >= 2:
                pl.wait("scalar", "sv", ln_done[tt - 2])
            pl.op("scalar", lambda e, s, xb=xb, tmp2s=tmp2s: e.activation(
                out=sqs[:, :], in_=xb, func=AF.Square, accum_out=tmp2s), inc="ss")
            s_sq = pl.val("ss")
            pl.wait("vector", "ss", s_sq)
            pl.op("vector", lambda e, s, tmp1s=tmp1s: e.tensor_scalar(
                out=tmp1s, in0=tmp1s, scalar1=1.0 / D, scalar2=None,
                op0=ALU.mult))
            pl.op("vector", lambda e, s, tmp2s=tmp2s: e.tensor_scalar(
                out=tmp2s, in0=tmp2s, scalar1=1.0 / D, scalar2=None,
                op0=ALU.mult))
            pl.op("vector", lambda e, s, tmp1s=tmp1s, tmp3s=tmp3s:
                  e.scalar_tensor_tensor(
                      out=tmp3s, in0=tmp1s, scalar=-1.0, in1=tmp1s,
                      op0=ALU.mult, op1=ALU.mult))
            pl.op("vector", lambda e, s, tmp2s=tmp2s, tmp3s=tmp3s: e.tensor_scalar(
                out=tmp2s, in0=tmp2s, scalar1=tmp3s, scalar2=1e-5,
                op0=ALU.add, op1=ALU.add), inc="sv")
            v_var = pl.val("sv")
            pl.wait("scalar", "sv", v_var)
            pl.op("scalar", lambda e, s, tmp2s=tmp2s: e.activation(
                out=tmp2s, in_=tmp2s, func=AF.Sqrt), inc="ss")
            pl.wait("vector", "ss", pl.val("ss"))
            pl.op("vector", lambda e, s, tmp2s=tmp2s: e.reciprocal(tmp2s, tmp2s))
            pl.op("vector", lambda e, s, tmp1s=tmp1s, tmp2s=tmp2s, tmp3s=tmp3s:
                  e.scalar_tensor_tensor(
                      out=tmp3s, in0=tmp1s, scalar=-1.0, in1=tmp2s,
                      op0=ALU.mult, op1=ALU.mult), inc="sv")
            v_nb = pl.val("sv")
            if tt >= 2:
                pl.wait("vector", "st", tr_last[tt - 2])
            pl.op("vector", lambda e, s, xb=xb, xs=xs, tmp2s=tmp2s, tmp3s=tmp3s:
                  e.tensor_scalar(
                      out=xs, in0=xb, scalar1=tmp2s, scalar2=tmp3s,
                      op0=ALU.mult, op1=ALU.add), inc="sv")
            ln_done[tt] = pl.val("sv")
            pl.wait("tensor", "sv", ln_done[tt])
            pl.wait("tensor", "sg", 2)
            for fp in range(8):
                fc = 2 * fp
                if len(tr_evac) >= 2:
                    pl.wait("tensor", "ss", tr_evac[-2])
                pb = pR0b if fp % 2 == 0 else pR1b
                pl.op("tensor", lambda e, s, xs=xs, fc=fc, pb=pb: e.transpose(
                    pb[:, 0:128],
                    xs[:, fc * 128:(fc + 1) * 128], ident[:, :]))
                pl.op("tensor", lambda e, s, xs=xs, fc=fc, pb=pb: e.transpose(
                    pb[:, 128:256],
                    xs[:, (fc + 1) * 128:(fc + 2) * 128], ident[:, :]), inc="st")
                t_tr = pl.val("st")
                pl.wait("scalar", "st", t_tr)
                pl.op("scalar", lambda e, s, fc=fc, tt=tt, pb=pb: e.activation(
                    out=xnT[:, :].rearrange("p (a b) -> p a b", b=T)
                    [:, fc:fc + 2, tt * 128:(tt + 1) * 128],
                    in_=pb[:, 0:256].rearrange("p (a b) -> p a b", b=128),
                    func=AF.Copy), inc="ss")
                tr_evac.append(pl.val("ss"))
            tr_last[tt] = pl.val("st")
        xn_ready = pl.val("ss")
        dbg_dxn_tok = 0
        if DEBUG:
            pl.wait("sync", "ss", xn_ready)
            dbg_dxn_tok = pl.dma("sync", dxn, xnT[:, :], "dbg")

        w_cons = [0, 0]

        def dequant_load(codes, scales, c0, bsl):
            cb = wc[:, bsl * 4096:(bsl + 1) * 4096]
            sbv = ws[:, bsl * 4096:(bsl + 1) * 4096]
            lw = "lw0" if bsl == 0 else "lw1"
            if w_cons[bsl]:
                pl.wait("sync", "st", w_cons[bsl])
            pl.dma("sync", cb.rearrange("p (a c) -> p a c", c=256),
                   codes[:, c0:c0 + 256].rearrange("(a p) c -> p a c", p=128), lw)
            pl.dma("sync", sbv.rearrange("p (a c) -> p a c", c=256),
                   scales[:, c0:c0 + 256].rearrange("(a p) c -> p a c", p=128), lw)
            t_w = pl.val(lw)
            pl.wait("vector", lw, t_w)
            pl.op("vector", lambda e, s, cb=cb, sbv=sbv:
                  e.tensor_mul(cb, cb, sbv), inc="sv")
            return cb, pl.val("sv")

        def qkv_mms(wb, ps, j):
            for hc in range(16):
                for half in range(2):
                    pl.op("tensor", lambda e, s, hc=hc, half=half, wb=wb, ps=ps, j=j:
                          e.matmul(ps[:, half * 512:(half + 1) * 512],
                                   wb[:, hc * 256 + j * 128: hc * 256 + (j + 1) * 128],
                                   xnT[:, hc * T + half * 512: hc * T + (half + 1) * 512],
                                   start=(hc == 0), stop=(hc == 15)),
                          inc="st" if (hc == 15 and half == 1) else None)
            return pl.val("st")

        nrm_v_prev = [0]

        def norm_mms(dst_row, h, s_ready, ksq_s):
            pl.wait("tensor", "ss", s_ready)
            if nrm_v_prev[0]:
                pl.wait("tensor", "sv", nrm_v_prev[0])
            for half in range(2):
                pl.op("tensor", lambda e, s, half=half, ksq_s=ksq_s: e.matmul(
                    (pR0 if half == 0 else pR1)[0:1, :], ones_b[:, :],
                    ksq_s[:, half * 512:(half + 1) * 512], start=True, stop=True),
                    inc="st" if half == 1 else None)
            t_n = pl.val("st")
            pl.wait("vector", "st", t_n)
            pl.op("vector", lambda e, s: e.reduce_max(
                nrm_a[0:1, 0:1], pR0[0:1, :], axis=AXX))
            pl.op("vector", lambda e, s: e.reduce_max(
                nrm_a[0:1, 1:2], pR1[0:1, :], axis=AXX))
            pl.op("vector", lambda e, s, h=h: e.reduce_max(
                dst_row[0:1, h:h + 1], nrm_a[0:1, :], axis=AXX),
                inc="sv")
            nrm_v_prev[0] = pl.val("sv")
            return t_n, pl.val("sv")

        norm_t_prev = 0
        pend_norm = None
        evac_tok = {}
        store_tok = {}
        for h in range(H):
            bsl = h % 2
            if h % 2 == 0:
                dq_pair = dequant_load(ck, sk, h * 128, (h // 2) % 2)
            wb, v_w = dq_pair
            pl.wait("tensor", "sv", v_w)
            pl.wait("tensor", "ss", xn_ready)
            if h >= 2:
                pl.wait("tensor", "sv", evac_tok[h - 2])
            ps = pP0 if h % 2 == 0 else pP1
            t_mm = qkv_mms(wb, ps, h % 2)
            if h % 2 == 1:
                w_cons[(h // 2) % 2] = t_mm
            kb = kstg[:, bsl * T:(bsl + 1) * T]
            if h >= 2:
                pl.wait("vector", "sk0" if bsl == 0 else "sk1",
                        store_tok[h - 2])
            pl.wait("vector", "st", t_mm)
            pl.op("vector", lambda e, s, ps=ps, kb=kb, h=h: e.tensor_scalar(
                out=kb, in0=ps[:, :], scalar1=bkc[:, h:h + 1], scalar2=None,
                op0=ALU.add), inc="sv")
            s_k = pl.val("sv")
            evac_tok[h] = s_k
            if pend_norm is not None:
                norm_t_prev, _ = norm_mms(kmx, pend_norm[0], pend_norm[1],
                                          pend_norm[2])
            if norm_t_prev:
                pl.wait("scalar", "st", norm_t_prev)
            pl.wait("scalar", "sv", s_k)
            ksq_s = ksq0 if h % 2 == 0 else ksq1
            pl.op("scalar", lambda e, s, kb=kb, ksq_s=ksq_s: e.activation(
                out=ksq_s[:, :], in_=kb, func=AF.Square), inc="ss")
            pend_norm = (h, pl.val("ss"), ksq_s)
            pl.wait("sync", "sv", s_k)
            store_tok[h] = pl.dma("sync", kvk_in[h * 128:(h + 1) * 128, :], kb,
                                  "sk0" if bsl == 0 else "sk1")
        norm_t_prev, _ = norm_mms(kmx, pend_norm[0], pend_norm[1], pend_norm[2])
        pend_norm = None
        k_stored = (pl.val("sk0"), pl.val("sk1"))
        kn_done = pl.val("sv")

        vt_evac = []
        prev_evac = [evac_tok[H - 2], evac_tok[H - 1]]
        evac_tok = {}
        store_tok = {}
        for h in range(H):
            bsl = h % 2
            if h % 2 == 0:
                dq_pair = dequant_load(cv, svs, h * 128, (h // 2) % 2)
            wb, v_w = dq_pair
            pl.wait("tensor", "sv", v_w)
            pl.wait("tensor", "sv", evac_tok[h - 2] if h >= 2 else prev_evac[h])
            ps = pP0 if h % 2 == 0 else pP1
            t_mm = qkv_mms(wb, ps, h % 2)
            if h % 2 == 1:
                w_cons[(h // 2) % 2] = t_mm
            vb = vstg[:, bsl * T:(bsl + 1) * T]
            vt = vts[:, bsl * T:(bsl + 1) * T]
            if h >= 2:
                pl.wait("vector", "sk0" if bsl == 0 else "sk1", store_tok[h - 2])
            pl.wait("vector", "st", t_mm)
            pl.op("vector", lambda e, s, ps=ps, vb=vb, h=h: e.tensor_scalar(
                out=vb, in0=ps[:, :], scalar1=bvc[:, h:h + 1], scalar2=None,
                op0=ALU.add), inc="sv")
            s_v = pl.val("sv")
            evac_tok[h] = s_v
            pl.wait("tensor", "sv", s_v)
            for jp in range(4):
                j = 2 * jp
                if len(vt_evac) >= 2:
                    pl.wait("tensor", "sv", vt_evac[-2])
                pb = pR0b if jp % 2 == 0 else pR1b
                pl.op("tensor", lambda e, s, vb=vb, j=j, pb=pb: e.transpose(
                    pb[:, 0:128],
                    vb[:, j * 128:(j + 1) * 128], ident[:, :]))
                pl.op("tensor", lambda e, s, vb=vb, j=j, pb=pb: e.transpose(
                    pb[:, 128:256],
                    vb[:, (j + 1) * 128:(j + 2) * 128], ident[:, :]), inc="st")
                t_tr = pl.val("st")
                pl.wait("vector", "st", t_tr)
                pl.op("vector", lambda e, s, vt=vt, j=j, pb=pb: e.tensor_copy(
                    out=vt[:, j * 128:(j + 2) * 128],
                    in_=pb[:, 0:256]), inc="sv")
                vt_evac.append(pl.val("sv"))
            pl.wait("sync", "sv", pl.val("sv"))
            store_tok[h] = pl.dma(
                "sync",
                kvv_in[h * T:(h + 1) * T, :].rearrange("(j p) d -> p j d", p=128),
                vt.rearrange("p (j d) -> p j d", d=DH),
                "sk0" if bsl == 0 else "sk1")
        prev_evac = [evac_tok[H - 2], evac_tok[H - 1]]
        evac_tok = {}
        for h in range(H):
            bsl = h % 2
            if h % 2 == 0:
                dq_pair = dequant_load(cq, sq, h * 128, (h // 2) % 2)
            wb, v_w = dq_pair
            pl.wait("tensor", "sv", v_w)
            pl.wait("tensor", "sv", evac_tok[h - 2] if h >= 2 else prev_evac[h])
            ps = pP0 if h % 2 == 0 else pP1
            t_mm = qkv_mms(wb, ps, h % 2)
            if h % 2 == 1:
                w_cons[(h // 2) % 2] = t_mm
            pl.wait("vector", "st", t_mm)
            pl.op("vector", lambda e, s, ps=ps, h=h: e.tensor_scalar(
                out=QT[:, h * T:(h + 1) * T], in0=ps[:, :], scalar1=bqc[:, h:h + 1],
                scalar2=None, op0=ALU.add), inc="sv")
            s_q = pl.val("sv")
            evac_tok[h] = s_q
            if pend_norm is not None:
                norm_t_prev, _ = norm_mms(mq2, pend_norm[0], pend_norm[1],
                                          pend_norm[2])
            if norm_t_prev:
                pl.wait("scalar", "st", norm_t_prev)
            pl.wait("scalar", "sv", s_q)
            ksq_s = ksq0 if h % 2 == 0 else ksq1
            pl.op("scalar", lambda e, s, h=h, ksq_s=ksq_s: e.activation(
                out=ksq_s[:, :], in_=QT[:, h * T:(h + 1) * T], func=AF.Square),
                inc="ss")
            pend_norm = (h, pl.val("ss"), ksq_s)
        norm_t_prev, _ = norm_mms(mq2, pend_norm[0], pend_norm[1], pend_norm[2])
        qn_done = pl.val("sv")
        pl.wait("sync", "sv", kn_done)
        pl.dma("sync", kvn_in.ap().rearrange("a b -> b a"), kmx[0:1, :], "skn", slow=True)
        pl.wait("gpsimd", "sk0", pl.val("sk0"))
        pl.wait("gpsimd", "sk1", pl.val("sk1"))
        pl.wait("gpsimd", "skn", pl.val("skn"))
        for (i_t, o_t) in ((kvk_in, kvk_out), (kvv_in, kvv_out), (kvn_in, kvn_out)):
            pl.op("gpsimd", lambda e, s, i_t=i_t, o_t=o_t: e.collective_compute(
                "AllGather", ALU.bypass,
                replica_groups=[[0, 1, 2, 3], [4, 5, 6, 7]],
                ins=[i_t.ap().opt()], outs=[o_t.ap().opt()]), inc="cc")
        cc_done = pl.val("cc")
        q_done_s = pl.val("ss")
        qkv_t_last = pl.val("st")
        qkv_evac_last = [evac_tok[H - 2], evac_tok[H - 1]]
        if DEBUG:
            pl.wait("sync", "sv", qn_done)
            pl.dma("sync", dqt, QT[:, :], "dbg")

        gu_cons = [0, 0]
        dw_cons = {}
        hT_done = {}
        silu_tok = {}
        uev_tok = {}
        pl.wait("scalar", "sk0", pl.val("sk0"))
        pl.wait("scalar", "sk1", pl.val("sk1"))
        pl.wait("gpsimd", "st", qkv_t_last)
        pl.wait("sync", "st", qkv_t_last)
        pl.wait("vector", "sk0", pl.val("sk0"))
        pl.wait("vector", "sk1", pl.val("sk1"))
        pair_w = {}
        down_done = {}

        def hts(blkd, ii):
            if ii >= 2:
                return hT[:, ii * T:(ii + 1) * T]
            base = hT if blkd % 2 == 0 else hTx
            return base[:, ii * T:(ii + 1) * T]

        rotd = [pP2[:, 0:512], pP2[:, 512:1024], pR0[:, :], pR1[:, :]]
        rot_tok = [0, 0, 0, 0]
        kdn = [0]

        def emit_down(blkd, lo, hi):
            pl.wait("tensor", "sv", hT_done[blkd * ITB + ITB - 1])
            for idx in range(lo, hi):
                tt, hc = idx // 4, idx % 4
                k = kdn[0]
                ps = rotd[k % 4]
                if rot_tok[k % 4]:
                    pl.wait("tensor", "sv", rot_tok[k % 4])
                for ii in range(ITB):
                    it = blkd * ITB + ii
                    dsl2 = (it // 2) % 3
                    jj2 = it % 2
                    pl.op("tensor",
                          lambda e, s, ps=ps, it=it, ii=ii, tt=tt, hc=hc,
                          dsl2=dsl2, jj2=jj2, blkd=blkd:
                          e.matmul(ps,
                                   hts(blkd, ii)[:, tt * 128:(tt + 1) * 128],
                                   dwb[:, dsl2 * 4096 + jj2 * 2048 + hc * 512:
                                       dsl2 * 4096 + jj2 * 2048 + (hc + 1) * 512],
                                   start=(ii == 0), stop=(ii == ITB - 1)),
                          inc="st" if ii == ITB - 1 else None)
                t_dn = pl.val("st")
                pl.wait("vector", "st", t_dn)
                fsl = facc[:, tt * D + hc * 512: tt * D + (hc + 1) * 512]
                if blkd == 0:
                    pl.op("vector", lambda e, s, fsl=fsl, ps=ps:
                          e.tensor_copy(out=fsl, in_=ps), inc="sv")
                else:
                    pl.op("vector", lambda e, s, fsl=fsl, ps=ps:
                          e.tensor_add(fsl, fsl, ps), inc="sv")
                rot_tok[k % 4] = pl.val("sv")
                kdn[0] += 1
            if hi == 32:
                tok = pl.val("st")
                for ii in range(ITB):
                    dw_cons[blkd * ITB + ii] = tok
                down_done[blkd] = tok

        for blk in range(NITB):
            for ii in range(ITB):
                it = blk * ITB + ii
                if it % 2 == 0:
                    pi = it // 2
                    psl_w = pi % 2
                    gwb2 = gw[:, psl_w * 4096:(psl_w + 1) * 4096]
                    uwb2 = uw[:, psl_w * 4096:(psl_w + 1) * 4096]
                    if gu_cons[psl_w]:
                        pl.wait("sync", "st", gu_cons[psl_w])
                    lw = "lw0" if psl_w == 0 else "lw1"
                    pl.dma("sync", gwb2.rearrange("p (a c) -> p a c", c=256),
                           wgT[:, it * 128:(it + 2) * 128]
                           .rearrange("(a p) c -> p a c", p=128), lw)
                    pl.dma("sync", uwb2.rearrange("p (a c) -> p a c", c=256),
                           wuT[:, it * 128:(it + 2) * 128]
                           .rearrange("(a p) c -> p a c", p=128), lw)
                    dsl2 = pi % 3
                    dwv2 = dwb[:, dsl2 * 4096:(dsl2 + 1) * 4096]
                    if pi >= 3:
                        pl.wait("sync", "st", dw_cons[(pi - 3) * 2])
                    pl.dma("sync", dwv2.rearrange("p (a c) -> p a c", c=2048),
                           wdT[it * 128:(it + 2) * 128, :]
                           .rearrange("(a p) c -> p a c", p=128), lw)
                    t_w = pl.val(lw)
                    pl.wait("vector", lw, t_w)
                    pl.op("vector", lambda e, s, gwb2=gwb2: e.tensor_scalar(
                        out=gwb2, in0=gwb2, scalar1=128.0, scalar2=None,
                        op0=ALU.subtract))
                    pl.op("vector", lambda e, s, uwb2=uwb2: e.tensor_scalar(
                        out=uwb2, in0=uwb2, scalar1=128.0, scalar2=None,
                        op0=ALU.subtract))
                    pl.op("vector", lambda e, s, dwv2=dwv2: e.tensor_scalar(
                        out=dwv2, in0=dwv2, scalar1=128.0, scalar2=None,
                        op0=ALU.subtract), inc="sv")
                    pair_w[pi] = (pl.val("sv"), gwb2, uwb2, dwv2)
                pi = it // 2
                jj = it % 2
                v_w, gwb2, uwb2, dwv2 = pair_w[pi]
                if blk >= 1 and ii < 2:
                    emit_down(blk - 1, ii * 16, (ii + 1) * 16)
                pl.wait("tensor", "sv", v_w)
                if it >= 1:
                    pl.wait("tensor", "ss", silu_tok[it - 1])
                    pl.wait("tensor", "sv", uev_tok[it - 1])
                else:
                    pl.wait("tensor", "sv", qkv_evac_last[1])
                for (wbv, ps) in ((gwb2, pP0), (uwb2, pP1)):
                    for hc in range(16):
                        for half in range(2):
                            pl.op("tensor",
                                  lambda e, s, wbv=wbv, ps=ps, hc=hc, half=half, jj=jj:
                                  e.matmul(ps[:, half * 512:(half + 1) * 512],
                                           wbv[:, hc * 256 + jj * 128: hc * 256 + (jj + 1) * 128],
                                           xnT[:, hc * T + half * 512: hc * T + (half + 1) * 512],
                                           start=(hc == 0), stop=(hc == 15)),
                                  inc="st" if (hc == 15 and half == 1) else None)
                if jj == 1:
                    gu_cons[pi % 2] = pl.val("st")
                t_gu = pl.val("st")
                gtile = kstg[:, 0:T]
                utile = kstg[:, T:2 * T]
                pl.wait("scalar", "st", t_gu)
                if it >= 1:
                    pl.wait("scalar", "sv", hT_done[it - 1])
                pl.op("scalar", lambda e, s, it=it, gtile=gtile: e.activation(
                    out=gtile, in_=pP0[:, :], func=AF.Silu,
                    scale=gscol[:, it:it + 1]), inc="ss")
                s_g = pl.val("ss")
                silu_tok[it] = s_g
                pl.wait("vector", "st", t_gu)
                pl.op("vector", lambda e, s, it=it, utile=utile: e.tensor_scalar(
                    out=utile, in0=pP1[:, :], scalar1=uscol[:, it:it + 1],
                    scalar2=None, op0=ALU.mult), inc="sv")
                uev_tok[it] = pl.val("sv")
                pl.wait("vector", "ss", s_g)
                if ii < 2 and blk >= 2:
                    pl.wait("vector", "st", down_done[blk - 2])
                pl.op("vector", lambda e, s, ii=ii, blk=blk, gtile=gtile, utile=utile:
                      e.tensor_mul(hts(blk, ii), gtile, utile),
                      inc="sv")
                hT_done[it] = pl.val("sv")
        emit_down(NITB - 1, 0, 32)
        ffn_t_last = pl.val("st")

        pl.wait("vector", "sv", qn_done)
        pl.wait("sync", "cc", cc_done)
        pl.dma("sync", knr[0:1, :], kvn_out[:, :].rearrange("a b -> b a"), "knl", slow=True)
        t_knr = pl.val("knl")
        pl.wait("vector", "knl", t_knr)
        pl.op("vector", lambda e, s: e.reduce_max(
            mk2[0:1, :].rearrange("a (h o) -> a h o", o=1),
            knr[0:1, :].rearrange("a (sh h) -> a h sh", h=H), axis=AXX), inc="sv")
        v_rows = pl.val("sv")
        pl.wait("tensor", "sv", v_rows)
        pl.op("tensor", lambda e, s: e.matmul(
            pR0[:, 0:H], ones_f[:, :], mq2[:, :], start=True, stop=True))
        pl.op("tensor", lambda e, s: e.matmul(
            pR1[:, 0:H], ones_f[:, :], mk2[:, :], start=True, stop=True), inc="st")
        t_rows = pl.val("st")
        pl.wait("vector", "st", t_rows)
        pl.op("vector", lambda e, s: e.tensor_copy(out=csb[:, :], in_=pR1[:, 0:H]))
        pl.op("vector", lambda e, s: e.tensor_mul(csb[:, :], pR0[:, 0:H], csb[:, :]),
              inc="sv")
        pl.wait("scalar", "sv", pl.val("sv"))
        pl.op("scalar", lambda e, s: e.activation(
            out=csb[:, :], in_=csb[:, :], func=AF.Sqrt), inc="ss")
        pl.wait("vector", "ss", pl.val("ss"))
        pl.op("vector", lambda e, s: e.tensor_scalar(
            out=csb[:, :], in0=csb[:, :], scalar1=-RSQ, scalar2=None,
            op0=ALU.mult), inc="sv")
        csb_done = pl.val("sv")
        pl.wait("tensor", "sv", pl.val("sv"))
        dsb_cp = {}
        for j in range(4):
            ps = pR0 if j % 2 == 0 else pR1
            pl.dma("sync", dsrow[:, :],
                   dsc[j * 512:(j + 1) * 512].rearrange("(o a) -> o a", o=1), "ldc")
            t_dsr = pl.val("ldc")
            pl.wait("tensor", "ldc", t_dsr)
            if j >= 2:
                pl.wait("tensor", "sv", dsb_cp[j - 2])
            pl.op("tensor", lambda e, s, j=j, ps=ps: e.matmul(
                ps[:, :], ones_f[:, :], dsrow[:, :],
                start=True, stop=True), inc="st")
            tv = pl.val("st")
            pl.wait("vector", "st", tv)
            dsb_cp[j] = pl.op("vector", lambda e, s, j=j, ps=ps: e.tensor_copy(
                out=dsb[:, j * 512:(j + 1) * 512], in_=ps[:, :]), inc="sv")
            if j < 3:
                pl.wait("sync", "st", tv)
        v_dsb = pl.val("sv")
        fx_done = {}
        pl.wait("vector", "sv", v_dsb)
        pl.wait("sync", "st", ffn_t_last)
        for tt in range(NT):
            xb = xbuf[:, 0:D]
            if tt >= 1:
                pl.wait("sync", "sv", fx_done[tt - 1])
            pl.dma("sync", xb, x_in[tt * 128:(tt + 1) * 128, :], "ldx")
            t_ld = pl.val("ldx")
            pl.wait("vector", "ldx", t_ld)
            fsl = facc[:, tt * D:(tt + 1) * D]
            pl.op("vector", lambda e, s, fsl=fsl: e.tensor_mul(fsl, fsl, dsb[:, :]))
            pl.op("vector", lambda e, s, fsl=fsl, xb=xb: e.tensor_add(fsl, fsl, xb),
                  inc="sv")
            fx_done[tt] = pl.val("sv")
        facc_done = pl.val("sv")
        if DEBUG:
            pl.wait("sync", "sv", csb_done)
            pl.dma("sync", dnr[0:1, 0:H], mq2[0:1, :], "dbg")
            pl.dma("sync", dnr[0:1, H:2 * H], mk2[0:1, :], "dbg")
            pl.dma("sync", dnr[0:1, 2 * H:3 * H], csb[0:1, :], "dbg")
            pl.dma("sync", dnr[0:1, 3 * H:3 * H + SHD * H], knr[0:1, :], "dbg")
            pl.dma("sync", dcs, csb[:, :], "dbg")
            pl.dma("sync", dkk, kvk_out[:, :], "dbg")
            pl.dma("sync", dkv, kvv_out[:, :], "dbg")
            pl.dma("sync", dkn, kvn_out[:, :], "dbg")
        pl.wait("scalar", "sv", csb_done)
        pl.wait("tensor", "sv", facc_done if False else v_dsb)
        pl.wait("tensor", "sv", csb_done)

        kv_cons = {}
        exp_done = {}
        av_done = {}
        s_done = {}
        rbs_tok = {}
        norm_tok = {}

        def head_finish(h):
            emit_avl(h, 31)
            t_av = av_done[31]
            pl.wait("vector", "st", t_av)
            pl.op("vector", lambda e, s: e.reciprocal(rrow[0:1, 0:512], pR0[0:1, :]))
            pl.op("vector", lambda e, s: e.reciprocal(rrow[0:1, 512:1024], pR1[0:1, :]),
                  inc="sv")
            v_r = pl.val("sv")
            pl.wait("tensor", "sv", v_r)
            pl.op("tensor", lambda e, s: e.matmul(
                pR0[:, :], ones_f[:, :], rrow[0:1, 0:512], start=True, stop=True))
            pl.op("tensor", lambda e, s: e.matmul(
                pR1[:, :], ones_f[:, :], rrow[0:1, 512:1024], start=True, stop=True),
                inc="st")
            t_rb = pl.val("st")
            pl.wait("vector", "st", t_rb)
            pl.op("vector", lambda e, s: e.tensor_copy(out=rbs[:, 0:512], in_=pR0[:, :]))
            pl.op("vector", lambda e, s: e.tensor_copy(out=rbs[:, 512:1024], in_=pR1[:, :]),
                  inc="sv")
            rbs_tok[h] = pl.val("sv")
            pl.op("vector", lambda e, s, h=h: e.tensor_mul(
                aoT[:, h * T:(h + 1) * T], pP2[:, :], rbs[:, :]), inc="sv")
            norm_tok[h] = pl.val("sv")

        def emit_avl(h, gkc):
            vbp = vbf[:, (gkc // 8 % 2) * T:((gkc // 8 % 2) + 1) * T]
            kc = gkc % 8
            pslot = (gkc % 2) * T
            pl.wait("tensor", "ss", exp_done[gkc])
            if gkc == 0 and h >= 1:
                pl.wait("tensor", "sv", rbs_tok[h - 1])
                pl.wait("tensor", "sv", norm_tok[h - 1])
            for half in range(2):
                pl.op("tensor",
                      lambda e, s, vbp=vbp, kc=kc, half=half, pslot=pslot, gkc=gkc:
                      e.matmul(pP2[:, half * 512:(half + 1) * 512],
                               vbp[:, kc * 128:(kc + 1) * 128],
                               ptb[:, pslot + half * 512: pslot + (half + 1) * 512],
                               start=(gkc == 0), stop=(gkc == 31)))
                pl.op("tensor", lambda e, s, half=half, pslot=pslot, gkc=gkc:
                      e.matmul((pR0 if half == 0 else pR1)[0:1, :],
                               ones_b[:, :],
                               ptb[:, pslot + half * 512: pslot + (half + 1) * 512],
                               start=(gkc == 0), stop=(gkc == 31)),
                      inc="st" if half == 1 else None)
            av_done[gkc] = pl.val("st")
            if gkc % 8 == 7:
                kv_cons[(h, gkc // 8)] = av_done[gkc]
        if DEBUG:
            pl.wait("vector", "dbg", pl.val("dbg"))
        pl.wait("sync", "st", ffn_t_last)
        for h in range(H):
            for sh in range(SHD):
                bsl = sh % 2
                kb = kbuf[:, bsl * T:(bsl + 1) * T]
                vb = vbf[:, bsl * T:(bsl + 1) * T]
                prev = (h, sh - 2) if sh >= 2 else ((h - 1, sh + 2) if h >= 1 else None)
                if prev in kv_cons:
                    pl.wait("sync", "st", kv_cons[prev])
                kvs = "kv0" if bsl == 0 else "kv1"
                pl.dma("sync", kb,
                       kvk_out[(sh * H + h) * DH:(sh * H + h + 1) * DH, :], kvs)
                pl.dma("sync", vb.rearrange("p (j d) -> p j d", d=DH),
                       kvv_out[(sh * H + h) * T:(sh * H + h + 1) * T, :]
                       .rearrange("(j p) d -> p j d", p=128), kvs)
                t_kv = pl.val(kvs)
                pl.wait("tensor", kvs, t_kv)
                for kc in range(8):
                    gkc = sh * 8 + kc
                    psl = pP0 if gkc % 2 == 0 else pP1
                    pslot = (gkc % 2) * T
                    if gkc >= 2:
                        pl.wait("tensor", "ss", exp_done[gkc - 2])
                    elif h >= 1 and gkc == 0:
                        pl.wait("tensor", "ss", exp_done[30])
                    elif h >= 1 and gkc == 1:
                        pl.wait("tensor", "ss", exp_done[31])
                    for half in range(2):
                        pl.op("tensor",
                              lambda e, s, kb=kb, kc=kc, h=h, half=half, psl=psl:
                              e.matmul(psl[:, half * 512:(half + 1) * 512],
                                       kb[:, kc * 128:(kc + 1) * 128],
                                       QT[:, h * T + half * 512: h * T + (half + 1) * 512],
                                       start=True, stop=True),
                              inc="st" if half == 1 else None)
                    s_done[gkc] = pl.val("st")
                    pl.wait("scalar", "st", s_done[gkc])
                    if gkc >= 2:
                        pl.wait("scalar", "st", av_done[gkc - 2])
                    pl.op("scalar", lambda e, s, psl=psl, pslot=pslot, h=h:
                          e.activation(out=ptb[:, pslot:pslot + T], in_=psl[:, :],
                                       func=AF.Exp, bias=csb[:, h:h + 1],
                                       scale=RSQ), inc="ss")
                    exp_done[gkc] = pl.val("ss")
                    if gkc >= 1:
                        emit_avl(h, gkc - 1)
                    elif h >= 1:
                        head_finish(h - 1)
        head_finish(H - 1)
        attn_done = pl.val("sv")
        if DEBUG:
            pl.wait("sync", "sv", attn_done)
            pl.dma("sync", dao, aoT[:, :], "dbg")
            pl.wait("sync", "sv", facc_done)
            pl.dma("sync", dfa, facc[:, :], "dbg")

        pl.wait("tensor", "sv", attn_done)
        pl.wait("tensor", "sv", facc_done)
        pl.wait("sync", "sv", attn_done)
        wo_cons = [0, 0]
        wo_tt = 0
        ost_tok = {}
        attn_t_last = pl.val("st")
        pl.wait("gpsimd", "st", attn_t_last)
        pl.wait("gpsimd", "sv", facc_done)
        pl.wait("sync", "st", attn_t_last)
        pl.wait("sync", "sv", facc_done)
        woscl = dAr[:, 0:8192]
        rot = [pP2[:, 0:512], pP2[:, 512:1024], pP0[:, 0:512], pP0[:, 512:1024]]
        rot_tok = [0, 0, 0, 0]
        k = 0
        bo_mm_tok = 0
        for dc in range(4):
            if bo_mm_tok:
                pl.wait("sync", "st", bo_mm_tok)
            pl.dma("sync", bor[:, :],
                   bo[dc * 512:(dc + 1) * 512].rearrange("(o a) -> o a", o=1), "ldc")
            t_bor = pl.val("ldc")
            pl.wait("tensor", "ldc", t_bor)
            wsl = dc % 2
            w0 = woc[:, wsl * 8192:(wsl + 1) * 8192]
            lws = "lw0" if wsl == 0 else "lw1"
            if wo_cons[wsl]:
                pl.wait("sync", "st", wo_cons[wsl])
            pl.dma("sync", w0.rearrange("p (a u c) -> p a u c", u=4, c=128),
                   co[:, dc * 512:(dc + 1) * 512]
                   .rearrange("(a p) (u c) -> p a u c", p=128, c=128), lws)
            if wo_tt:
                pl.wait("sync", "sv", wo_tt)
            pl.dma("sync", woscl.rearrange("p (a u c) -> p a u c", u=4, c=128),
                   so[:, dc * 512:(dc + 1) * 512]
                   .rearrange("(a p) (u c) -> p a u c", p=128, c=128), lws)
            t_wo = pl.val(lws)
            pl.wait("vector", lws, t_wo)
            pl.op("vector", lambda e, s, w0=w0: e.tensor_mul(w0, w0, woscl),
                  inc="sv")
            v_wo = pl.val("sv")
            wo_tt = v_wo
            pl.wait("tensor", "sv", v_wo)
            for tt in range(NT):
                ps = rot[k % 4]
                if rot_tok[k % 4]:
                    pl.wait("tensor", "sv", rot_tok[k % 4])
                for fc in range(16):
                    pl.op("tensor", lambda e, s, ps=ps, fc=fc, tt=tt, w0=w0:
                          e.matmul(ps,
                                   aoT[:, fc * T + tt * 128: fc * T + (tt + 1) * 128],
                                   w0[:, fc * 512:(fc + 1) * 512],
                                   start=(fc == 0), stop=False))
                pl.op("tensor", lambda e, s, ps=ps: e.matmul(
                    ps, ones_f[:, :], bor[0:1, :],
                    start=False, stop=True), inc="st")
                t_op = pl.val("st")
                bo_mm_tok = t_op
                osl = outs[:, (k % 2) * 512:(k % 2 + 1) * 512]
                if k >= 2:
                    pl.wait("vector", "st0" if k % 2 == 0 else "st1",
                            ost_tok[k - 2])
                pl.wait("vector", "st", t_op)
                pl.op("vector", lambda e, s, ps=ps, osl=osl, tt=tt, dc=dc:
                      e.tensor_add(osl, ps,
                                   facc[:, tt * D + dc * 512: tt * D + (dc + 1) * 512]),
                      inc="sv")
                rot_tok[k % 4] = pl.val("sv")
                pl.wait("sync", "sv", rot_tok[k % 4])
                ost_tok[k] = pl.dma(
                    "sync",
                    y_out[tt * 128:(tt + 1) * 128, dc * 512:(dc + 1) * 512],
                    osl, "st0" if k % 2 == 0 else "st1")
                k += 1
            wo_cons[wsl] = pl.val("st")
        pl.wait("sync", "st0", pl.val("st0"))
        pl.wait("sync", "st1", pl.val("st1"))

        block = ctx.enter_context(nc.Block())
        S_ = sems

        @block.sync
        def _(e):
            for fn in pl.q["sync"]:
                fn(e, S_)

        @block.gpsimd
        def _(e):
            for fn in pl.q["gpsimd"]:
                fn(e, S_)

        @block.vector
        def _(e):
            for fn in pl.q["vector"]:
                fn(e, S_)

        @block.scalar
        def _(e):
            for fn in pl.q["scalar"]:
                fn(e, S_)

        @block.tensor
        def _(e):
            for fn in pl.q["tensor"]:
                fn(e, S_)

    return nc


_NC_CACHE = None


def kernel(**inputs):
    global _NC_CACHE
    x = np.asarray(inputs["L0_attn_x"])
    x2d = np.ascontiguousarray(x.reshape(B * S, D))
    bf = ml_dtypes.bfloat16
    rep = lambda s: np.ascontiguousarray(
        np.repeat(np.asarray(s), 32, axis=0).astype(bf))
    base = {
        "cq": np.ascontiguousarray(np.asarray(inputs["L0_attn_wq_codes"]).astype(bf)),
        "ck": np.ascontiguousarray(np.asarray(inputs["L0_attn_wk_codes"]).astype(bf)),
        "cv": np.ascontiguousarray(np.asarray(inputs["L0_attn_wv_codes"]).astype(bf)),
        "co": np.ascontiguousarray(np.asarray(inputs["L0_attn_wo_codes"]).astype(bf)),
        "sq": rep(inputs["L0_attn_wq_scales"]),
        "sk": rep(inputs["L0_attn_wk_scales"]),
        "svs": rep(inputs["L0_attn_wv_scales"]),
        "so": rep(inputs["L0_attn_wo_scales"]),
        "bq": np.ascontiguousarray(np.asarray(inputs["L0_attn_bq"])),
        "bqcd": np.ascontiguousarray(np.asarray(inputs["L0_attn_bq"]).reshape(16, 128).T),
        "bkcd": np.ascontiguousarray(np.asarray(inputs["L0_attn_bk"]).reshape(16, 128).T),
        "bvcd": np.ascontiguousarray(np.asarray(inputs["L0_attn_bv"]).reshape(16, 128).T),
        "gscd": np.ascontiguousarray(np.asarray(inputs["L0_ffn_gate_scale"]).reshape(64, 128).T),
        "uscd": np.ascontiguousarray(np.asarray(inputs["L0_ffn_up_scale"]).reshape(64, 128).T),
        "bk": np.ascontiguousarray(np.asarray(inputs["L0_attn_bk"])),
        "bv": np.ascontiguousarray(np.asarray(inputs["L0_attn_bv"])),
        "bo": np.ascontiguousarray(np.asarray(inputs["L0_attn_bo"])),
        "wgT": np.ascontiguousarray(np.asarray(inputs["L0_ffn_gate_w"]).T.astype(bf)),
        "wuT": np.ascontiguousarray(np.asarray(inputs["L0_ffn_up_w"]).T.astype(bf)),
        "wdT": np.ascontiguousarray(np.asarray(inputs["L0_ffn_down_w"]).T.astype(bf)),
        "gsc": np.ascontiguousarray(np.asarray(inputs["L0_ffn_gate_scale"])),
        "usc": np.ascontiguousarray(np.asarray(inputs["L0_ffn_up_scale"])),
        "dsc": np.ascontiguousarray(np.asarray(inputs["L0_ffn_down_scale"])),
    }
    in_maps = []
    for c in range(NCORES):
        m = dict(base)
        m["x"] = np.ascontiguousarray(x2d[c * T:(c + 1) * T, :])
        in_maps.append(m)
    if _NC_CACHE is None:
        _NC_CACHE = build_nc()
    res = run_bass_kernel_spmd(_NC_CACHE, in_maps, core_ids=list(range(NCORES)))
    out = np.concatenate([np.asarray(r["y"]) for r in res.results], axis=0)
    return out.reshape(B, S, D)


# version 19 (final)
# speedup vs baseline: 1.1316x; 1.1079x over previous
import numpy as np
import ml_dtypes
import concourse.bass as bass
import concourse.mybir as mybir
from concourse.bass_utils import run_bass_kernel_spmd

F32 = mybir.dt.float32
BF16 = mybir.dt.bfloat16
I32 = mybir.dt.int32
AF = mybir.ActivationFunctionType
ALU = mybir.AluOpType
AXX = mybir.AxisListType.X

DEBUG = False
NCORES = 8
B, S, D = 2, 4096, 2048
H, DH = 16, 128
FI = 8192
T = 1024
NT = T // 128
SHD = 4
RSQ = float(1.0 / np.sqrt(128.0))
ITB = 4
NITB = (FI // 128) // ITB


class Plan:

    def __init__(self):
        self.q = {k: [] for k in ("sync", "gpsimd", "vector", "scalar", "tensor")}
        self.cnt = {}

    def sem(self, name):
        self.cnt.setdefault(name, 0)

    def val(self, name):
        return self.cnt[name]

    def op(self, eng, fn, inc=None):
        if eng == "vector" or eng == "scalar":
            auto = "sv" if eng == "vector" else "ss"
            prev = self.cnt[auto]
            if prev > 0:
                self.q[eng].append(
                    lambda e, s, sem=auto, v=prev: e.wait_ge(s[sem], v))
            self.cnt[auto] += 1
            self.q[eng].append(
                lambda e, s, sem=auto: fn(e, s).then_inc(s[sem], 1))
            return self.cnt[auto]
        if inc is None:
            self.q[eng].append(lambda e, s: fn(e, s))
            return None
        self.cnt[inc] += 1
        self.q[eng].append(lambda e, s, inc=inc: fn(e, s).then_inc(s[inc], 1))
        return self.cnt[inc]

    def dma(self, eng, out, in_, sem, slow=False):
        self.cnt[sem] += 16
        self.q[eng].append(lambda e, s, out=out, in_=in_, sem=sem, slow=slow:
                           e.dma_start(out=out, in_=in_,
                                       allow_slow_non_contiguous=slow)
                           .then_inc(s[sem], 16))
        return self.cnt[sem]

    def wait(self, eng, sem, v=None):
        if v is None:
            v = self.cnt[sem]
        if v <= 0:
            return
        self.q[eng].append(lambda e, s, sem=sem, v=v: e.wait_ge(s[sem], v))


def build_nc():
    nc = bass.Bass("TRN2", target_bir_lowering=False, debug=False)

    din = lambda n, sh, dt: nc.dram_tensor(n, sh, dt, kind="ExternalInput").ap()
    x_in = din("x", [T, D], F32)
    cq, ck, cv, co = (din(n, [D, D], BF16) for n in ("cq", "ck", "cv", "co"))
    sq, sk, svs, so = (din(n, [D, D], BF16) for n in ("sq", "sk", "svs", "so"))
    bq, bk, bv, bo = (din(n, [D], F32) for n in ("bq", "bk", "bv", "bo"))
    bqcd = din("bqcd", [128, H], F32)
    bkcd = din("bkcd", [128, H], F32)
    bvcd = din("bvcd", [128, H], F32)
    gscd = din("gscd", [128, FI // 128], F32)
    uscd = din("uscd", [128, FI // 128], F32)
    wgT = din("wgT", [D, FI], BF16)
    wuT = din("wuT", [D, FI], BF16)
    wdT = din("wdT", [FI, D], BF16)
    gsc = din("gsc", [FI], F32)
    usc = din("usc", [FI], F32)
    dsc = din("dsc", [D], F32)
    y_out = nc.dram_tensor("y", [T, D], F32, kind="ExternalOutput").ap()
    if DEBUG:
        dxn = nc.dram_tensor("dxn", [128, H * T], BF16, kind="ExternalOutput").ap()
        dqt = nc.dram_tensor("dqt", [128, H * T], BF16, kind="ExternalOutput").ap()
        dkk = nc.dram_tensor("dkk", [SHD * H * DH, T], BF16, kind="ExternalOutput").ap()
        dkv = nc.dram_tensor("dkv", [SHD * H * T, DH], BF16, kind="ExternalOutput").ap()
        dkn = nc.dram_tensor("dkn", [SHD * H, 1], F32, kind="ExternalOutput").ap()
        dcs = nc.dram_tensor("dcs", [128, H], F32, kind="ExternalOutput").ap()
        dao = nc.dram_tensor("dao", [128, H * T], BF16, kind="ExternalOutput").ap()
        dfa = nc.dram_tensor("dfa", [128, NT * D], F32, kind="ExternalOutput").ap()
        dnr = nc.dram_tensor("dnr", [1, 3 * H + SHD * H], F32, kind="ExternalOutput").ap()

    kvk_in = nc.dram_tensor("kvk_in", [H * DH, T], BF16)
    kvv_in = nc.dram_tensor("kvv_in", [H * T, DH], BF16)
    kvn_in = nc.dram_tensor("kvn_in", [H, 1], F32)
    kvk_out = nc.dram_tensor("kvk_out", [SHD * H * DH, T], BF16)
    kvv_out = nc.dram_tensor("kvv_out", [SHD * H * T, DH], BF16)
    kvn_out = nc.dram_tensor("kvn_out", [SHD * H, 1], F32)

    from contextlib import ExitStack
    with ExitStack() as ctx:
        sb = lambda n, shp, dt: ctx.enter_context(nc.sbuf_tensor(n, shp, dt))
        xnT = sb("xnT", [128, H * T], BF16)
        aoT = xnT
        QT = sb("QT", [128, H * T], BF16)
        facc = sb("facc", [128, NT * D], F32)
        wAr = sb("wAr", [128, 16384], BF16)
        wc = wAr[:, 0:8192]
        ws = wAr[:, 8192:16384]
        gw = wAr[:, 0:8192]
        uw = wAr[:, 8192:16384]
        woc = wAr[:, 0:16384]
        dAr = sb("dAr", [128, (ITB + 2) * D], BF16)
        dwb = dAr[:, :]
        kbuf = dAr[:, 0:2048]
        vbf = dAr[:, 2048:4096]
        ptb = dAr[:, 4096:6144]
        xbuf = dAr[:, 6144:10240].bitcast(F32)
        aAr = sb("aAr", [128, 2 * D], BF16)
        xnstg = aAr[:, :]
        vstg = aAr[:, 0:2048]
        vts = aAr[:, 2048:4096]
        hT = aAr[:, :]
        ksq0 = aAr[:, 0:1024]
        ksq1 = aAr[:, 1024:2048]
        sqs = None
        kstg = sb("kstg", [128, 2 * T], BF16)
        rbs = kstg[:, :].bitcast(F32)[:, 0:1024]
        outs = aAr[:, :].bitcast(F32)[:, 0:1024]
        ident = sb("ident", [128, 128], BF16)
        ones_b = sb("ones_b", [128, 1], BF16)
        ones_f = sb("ones_f", [1, 128], F32)
        bqc = sb("bqc", [128, H], F32)
        bkc = sb("bkc", [128, H], F32)
        bvc = sb("bvc", [128, H], F32)
        gscol = sb("gscol", [128, FI // 128], F32)
        uscol = sb("uscol", [128, FI // 128], F32)
        hTx = sb("hTx", [128, 2048], BF16)
        rowA = sb("rowA", [1, T], F32)
        dsrow = rowA[0:1, 0:512]
        rrow = rowA[0:1, 0:T]
        bor = rowA[0:1, 0:512]
        dsb = wAr[:, 0:4096].bitcast(F32)
        csb = sb("csb", [128, H], F32)
        knr = sb("knr", [1, SHD * H], F32)
        knc = sb("knc", [SHD * H, 1], F32)
        mq2 = sb("mq2", [1, H], F32)
        kmx = sb("kmx", [1, H], F32)
        mk2 = sb("mk2", [1, H], F32)
        crow = sb("crow", [1, H], F32)
        nrm_a = sb("nrm_a", [1, 2], F32)
        tmp1 = sb("tmp1", [128, 2], F32)
        tmp2 = sb("tmp2", [128, 2], F32)
        tmp3 = sb("tmp3", [128, 2], F32)

        pP0 = ctx.enter_context(nc.psum_tensor("pP0", [128, 1024], F32))
        pP1 = ctx.enter_context(nc.psum_tensor("pP1", [128, 1024], F32))
        pP2 = ctx.enter_context(nc.psum_tensor("pP2", [128, 1024], F32))
        pR0 = ctx.enter_context(nc.psum_tensor("pR0", [128, 512], F32))
        pR1 = ctx.enter_context(nc.psum_tensor("pR1", [128, 512], F32))
        pR0b = pR0[:, :].bitcast(BF16)
        pR1b = pR1[:, :].bitcast(BF16)

        sem_names = ["ldx", "lx1", "ldc", "ldg", "lw0", "lw1", "kv0", "kv1",
                     "sk0", "sk1", "skn", "knl", "st0", "st1", "dbg", "cc", "sv", "ss", "st", "sg"]
        sems = {n: ctx.enter_context(nc.semaphore(n)) for n in sem_names}
        pl = Plan()
        for n in sem_names:
            pl.sem(n)
        sqs = facc[:, 0:D]

        pl.dma("gpsimd", bqc[:, :], bqcd, "ldg")
        pl.dma("gpsimd", bkc[:, :], bkcd, "ldg")
        pl.dma("gpsimd", bvc[:, :], bvcd, "ldg")
        pl.dma("gpsimd", gscol[:, :], gscd, "ldg")
        pl.dma("gpsimd", uscol[:, :], uscd, "ldg")
        tok_const = pl.val("ldg")
        pl.op("gpsimd", lambda e, s: e.memset(ident[:, :], 0.0), inc="sg")
        pl.wait("gpsimd", "sg", pl.val("sg"))
        pl.op("gpsimd", lambda e, s: e.affine_select(
            out=ident[:, :], in_=ident[:, :], compare_op=ALU.not_equal,
            fill=1.0, base=0, pattern=[[-1, 128]], channel_multiplier=1), inc="sg")
        pl.op("vector", lambda e, s: e.memset(ones_b[:, :], 1.0))
        pl.op("vector", lambda e, s: e.memset(ones_f[:, :], 1.0), inc="sv")
        v_ones = pl.val("sv")
        pl.wait("tensor", "sv", v_ones)
        pl.wait("tensor", "ldg", tok_const)
        pl.wait("vector", "ldg", tok_const)
        pl.wait("scalar", "ldg", tok_const)

        ln_done = {}
        tr_last = {}
        tr_evac = []
        for tt in range(NT):
            sl = tt % 2
            xb = xbuf[:, 0:D] if sl == 0 else facc[:, D:2 * D]
            xs = xnstg[:, sl * D:(sl + 1) * D]
            tmp1s = tmp1[:, sl:sl + 1]
            tmp2s = tmp2[:, sl:sl + 1]
            tmp3s = tmp3[:, sl:sl + 1]
            if tt >= 2:
                pl.wait("sync", "sv", ln_done[tt - 2])
            lxs = "ldx" if sl == 0 else "lx1"
            pl.dma("sync", xb, x_in[tt * 128:(tt + 1) * 128, :], lxs)
            t_ld = pl.val(lxs)
            pl.wait("vector", lxs, t_ld)
            pl.op("vector", lambda e, s, xb=xb, tmp1s=tmp1s: e.reduce_sum(
                tmp1s, xb, axis=AXX), inc="sv")
            pl.wait("scalar", lxs, t_ld)
            if tt >= 2:
                pl.wait("scalar", "sv", ln_done[tt - 2])
            pl.op("scalar", lambda e, s, xb=xb, tmp2s=tmp2s: e.activation(
                out=sqs[:, :], in_=xb, func=AF.Square, accum_out=tmp2s), inc="ss")
            s_sq = pl.val("ss")
            pl.wait("vector", "ss", s_sq)
            pl.op("vector", lambda e, s, tmp1s=tmp1s: e.tensor_scalar(
                out=tmp1s, in0=tmp1s, scalar1=1.0 / D, scalar2=None,
                op0=ALU.mult))
            pl.op("vector", lambda e, s, tmp2s=tmp2s: e.tensor_scalar(
                out=tmp2s, in0=tmp2s, scalar1=1.0 / D, scalar2=None,
                op0=ALU.mult))
            pl.op("vector", lambda e, s, tmp1s=tmp1s, tmp3s=tmp3s:
                  e.scalar_tensor_tensor(
                      out=tmp3s, in0=tmp1s, scalar=-1.0, in1=tmp1s,
                      op0=ALU.mult, op1=ALU.mult))
            pl.op("vector", lambda e, s, tmp2s=tmp2s, tmp3s=tmp3s: e.tensor_scalar(
                out=tmp2s, in0=tmp2s, scalar1=tmp3s, scalar2=1e-5,
                op0=ALU.add, op1=ALU.add), inc="sv")
            v_var = pl.val("sv")
            pl.wait("scalar", "sv", v_var)
            pl.op("scalar", lambda e, s, tmp2s=tmp2s: e.activation(
                out=tmp2s, in_=tmp2s, func=AF.Sqrt), inc="ss")
            pl.wait("vector", "ss", pl.val("ss"))
            pl.op("vector", lambda e, s, tmp2s=tmp2s: e.reciprocal(tmp2s, tmp2s))
            pl.op("vector", lambda e, s, tmp1s=tmp1s, tmp2s=tmp2s, tmp3s=tmp3s:
                  e.scalar_tensor_tensor(
                      out=tmp3s, in0=tmp1s, scalar=-1.0, in1=tmp2s,
                      op0=ALU.mult, op1=ALU.mult), inc="sv")
            v_nb = pl.val("sv")
            if tt >= 2:
                pl.wait("vector", "st", tr_last[tt - 2])
            pl.op("vector", lambda e, s, xb=xb, xs=xs, tmp2s=tmp2s, tmp3s=tmp3s:
                  e.tensor_scalar(
                      out=xs, in0=xb, scalar1=tmp2s, scalar2=tmp3s,
                      op0=ALU.mult, op1=ALU.add), inc="sv")
            ln_done[tt] = pl.val("sv")
            pl.wait("tensor", "sv", ln_done[tt])
            pl.wait("tensor", "sg", 2)
            for fp in range(8):
                fc = 2 * fp
                if len(tr_evac) >= 2:
                    pl.wait("tensor", "ss", tr_evac[-2])
                pb = pR0b if fp % 2 == 0 else pR1b
                pl.op("tensor", lambda e, s, xs=xs, fc=fc, pb=pb: e.transpose(
                    pb[:, 0:128],
                    xs[:, fc * 128:(fc + 1) * 128], ident[:, :]))
                pl.op("tensor", lambda e, s, xs=xs, fc=fc, pb=pb: e.transpose(
                    pb[:, 128:256],
                    xs[:, (fc + 1) * 128:(fc + 2) * 128], ident[:, :]), inc="st")
                t_tr = pl.val("st")
                pl.wait("scalar", "st", t_tr)
                pl.op("scalar", lambda e, s, fc=fc, tt=tt, pb=pb: e.activation(
                    out=xnT[:, :].rearrange("p (a b) -> p a b", b=T)
                    [:, fc:fc + 2, tt * 128:(tt + 1) * 128],
                    in_=pb[:, 0:256].rearrange("p (a b) -> p a b", b=128),
                    func=AF.Copy), inc="ss")
                tr_evac.append(pl.val("ss"))
            tr_last[tt] = pl.val("st")
        xn_ready = pl.val("ss")
        dbg_dxn_tok = 0
        if DEBUG:
            pl.wait("sync", "ss", xn_ready)
            dbg_dxn_tok = pl.dma("sync", dxn, xnT[:, :], "dbg")

        w_cons = [0, 0]
        dq_jobs = [(m_c, m_s, p * 256)
                   for (m_c, m_s) in ((ck, sk), (cv, svs), (cq, sq))
                   for p in range(8)]
        dq_res = {}

        def dq_issue(j):
            if j >= len(dq_jobs) or j in dq_res:
                return
            codes, scales, c0 = dq_jobs[j]
            bsl = j % 2
            cb = wc[:, bsl * 4096:(bsl + 1) * 4096]
            sbv = ws[:, bsl * 4096:(bsl + 1) * 4096]
            lw = "lw0" if bsl == 0 else "lw1"
            if w_cons[bsl]:
                pl.wait("sync", "st", w_cons[bsl])
            pl.dma("sync", cb.rearrange("p (a c) -> p a c", c=256),
                   codes[:, c0:c0 + 256].rearrange("(a p) c -> p a c", p=128), lw)
            pl.dma("sync", sbv.rearrange("p (a c) -> p a c", c=256),
                   scales[:, c0:c0 + 256].rearrange("(a p) c -> p a c", p=128), lw)
            t_w = pl.val(lw)
            pl.wait("vector", lw, t_w)
            v = pl.op("vector", lambda e, s, cb=cb, sbv=sbv:
                      e.tensor_mul(cb, cb, sbv))
            dq_res[j] = (cb, v)

        def qkv_mms(wb, ps, j):
            for hc in range(16):
                for half in range(2):
                    pl.op("tensor", lambda e, s, hc=hc, half=half, wb=wb, ps=ps, j=j:
                          e.matmul(ps[:, half * 512:(half + 1) * 512],
                                   wb[:, hc * 256 + j * 128: hc * 256 + (j + 1) * 128],
                                   xnT[:, hc * T + half * 512: hc * T + (half + 1) * 512],
                                   start=(hc == 0), stop=(hc == 15)),
                          inc="st" if (hc == 15 and half == 1) else None)
            return pl.val("st")

        nrm_v_prev = [0]

        def norm_mms(dst_row, h, s_ready, ksq_s):
            pl.wait("tensor", "ss", s_ready)
            if nrm_v_prev[0]:
                pl.wait("tensor", "sv", nrm_v_prev[0])
            for half in range(2):
                pl.op("tensor", lambda e, s, half=half, ksq_s=ksq_s: e.matmul(
                    (pR0 if half == 0 else pR1)[0:1, :], ones_b[:, :],
                    ksq_s[:, half * 512:(half + 1) * 512], start=True, stop=True),
                    inc="st" if half == 1 else None)
            t_n = pl.val("st")
            pl.wait("vector", "st", t_n)
            pl.op("vector", lambda e, s: e.reduce_max(
                nrm_a[0:1, 0:1], pR0[0:1, :], axis=AXX))
            pl.op("vector", lambda e, s: e.reduce_max(
                nrm_a[0:1, 1:2], pR1[0:1, :], axis=AXX))
            pl.op("vector", lambda e, s, h=h: e.reduce_max(
                dst_row[0:1, h:h + 1], nrm_a[0:1, :], axis=AXX),
                inc="sv")
            nrm_v_prev[0] = pl.val("sv")
            return t_n, pl.val("sv")

        norm_t_prev = 0
        pend_norm = None
        evac_tok = {}
        store_tok = {}
        dq_issue(0)
        for h in range(H):
            bsl = h % 2
            if h % 2 == 0:
                dq_issue(h // 2 + 1)
                dq_pair = dq_res[h // 2]
            wb, v_w = dq_pair
            pl.wait("tensor", "sv", v_w)
            pl.wait("tensor", "ss", xn_ready)
            if h >= 2:
                pl.wait("tensor", "sv", evac_tok[h - 2])
            ps = pP0 if h % 2 == 0 else pP1
            t_mm = qkv_mms(wb, ps, h % 2)
            if h % 2 == 1:
                w_cons[(h // 2) % 2] = t_mm
            kb = kstg[:, bsl * T:(bsl + 1) * T]
            if h >= 2:
                pl.wait("vector", "sk0" if bsl == 0 else "sk1",
                        store_tok[h - 2])
            pl.wait("vector", "st", t_mm)
            pl.op("vector", lambda e, s, ps=ps, kb=kb, h=h: e.tensor_scalar(
                out=kb, in0=ps[:, :], scalar1=bkc[:, h:h + 1], scalar2=None,
                op0=ALU.add), inc="sv")
            s_k = pl.val("sv")
            evac_tok[h] = s_k
            if pend_norm is not None:
                norm_t_prev, _ = norm_mms(kmx, pend_norm[0], pend_norm[1],
                                          pend_norm[2])
            if norm_t_prev:
                pl.wait("scalar", "st", norm_t_prev)
            pl.wait("scalar", "sv", s_k)
            ksq_s = ksq0 if h % 2 == 0 else ksq1
            pl.op("scalar", lambda e, s, kb=kb, ksq_s=ksq_s: e.activation(
                out=ksq_s[:, :], in_=kb, func=AF.Square), inc="ss")
            pend_norm = (h, pl.val("ss"), ksq_s)
            pl.wait("sync", "sv", s_k)
            store_tok[h] = pl.dma("sync", kvk_in[h * 128:(h + 1) * 128, :], kb,
                                  "sk0" if bsl == 0 else "sk1")
        norm_t_prev, _ = norm_mms(kmx, pend_norm[0], pend_norm[1], pend_norm[2])
        pend_norm = None
        k_stored = (pl.val("sk0"), pl.val("sk1"))
        kn_done = pl.val("sv")

        vt_evac = []
        prev_evac = [evac_tok[H - 2], evac_tok[H - 1]]
        evac_tok = {}
        store_tok = {}
        for h in range(H):
            bsl = h % 2
            if h % 2 == 0:
                dq_issue(8 + h // 2 + 1)
                dq_pair = dq_res[8 + h // 2]
            wb, v_w = dq_pair
            pl.wait("tensor", "sv", v_w)
            pl.wait("tensor", "sv", evac_tok[h - 2] if h >= 2 else prev_evac[h])
            ps = pP0 if h % 2 == 0 else pP1
            t_mm = qkv_mms(wb, ps, h % 2)
            if h % 2 == 1:
                w_cons[(h // 2) % 2] = t_mm
            vb = vstg[:, bsl * T:(bsl + 1) * T]
            vt = vts[:, bsl * T:(bsl + 1) * T]
            if h >= 2:
                pl.wait("vector", "sk0" if bsl == 0 else "sk1", store_tok[h - 2])
            pl.wait("vector", "st", t_mm)
            pl.op("vector", lambda e, s, ps=ps, vb=vb, h=h: e.tensor_scalar(
                out=vb, in0=ps[:, :], scalar1=bvc[:, h:h + 1], scalar2=None,
                op0=ALU.add), inc="sv")
            s_v = pl.val("sv")
            evac_tok[h] = s_v
            pl.wait("tensor", "sv", s_v)
            for jp in range(4):
                j = 2 * jp
                if len(vt_evac) >= 2:
                    pl.wait("tensor", "sv", vt_evac[-2])
                pb = pR0b if jp % 2 == 0 else pR1b
                pl.op("tensor", lambda e, s, vb=vb, j=j, pb=pb: e.transpose(
                    pb[:, 0:128],
                    vb[:, j * 128:(j + 1) * 128], ident[:, :]))
                pl.op("tensor", lambda e, s, vb=vb, j=j, pb=pb: e.transpose(
                    pb[:, 128:256],
                    vb[:, (j + 1) * 128:(j + 2) * 128], ident[:, :]), inc="st")
                t_tr = pl.val("st")
                pl.wait("vector", "st", t_tr)
                pl.op("vector", lambda e, s, vt=vt, j=j, pb=pb: e.tensor_copy(
                    out=vt[:, j * 128:(j + 2) * 128],
                    in_=pb[:, 0:256]), inc="sv")
                vt_evac.append(pl.val("sv"))
            pl.wait("sync", "sv", pl.val("sv"))
            store_tok[h] = pl.dma(
                "sync",
                kvv_in[h * T:(h + 1) * T, :].rearrange("(j p) d -> p j d", p=128),
                vt.rearrange("p (j d) -> p j d", d=DH),
                "sk0" if bsl == 0 else "sk1")
        prev_evac = [evac_tok[H - 2], evac_tok[H - 1]]
        evac_tok = {}
        for h in range(H):
            bsl = h % 2
            if h % 2 == 0:
                dq_issue(16 + h // 2 + 1)
                dq_pair = dq_res[16 + h // 2]
            wb, v_w = dq_pair
            pl.wait("tensor", "sv", v_w)
            pl.wait("tensor", "sv", evac_tok[h - 2] if h >= 2 else prev_evac[h])
            ps = pP0 if h % 2 == 0 else pP1
            t_mm = qkv_mms(wb, ps, h % 2)
            if h % 2 == 1:
                w_cons[(h // 2) % 2] = t_mm
            pl.wait("vector", "st", t_mm)
            pl.op("vector", lambda e, s, ps=ps, h=h: e.tensor_scalar(
                out=QT[:, h * T:(h + 1) * T], in0=ps[:, :], scalar1=bqc[:, h:h + 1],
                scalar2=None, op0=ALU.add), inc="sv")
            s_q = pl.val("sv")
            evac_tok[h] = s_q
            if pend_norm is not None:
                norm_t_prev, _ = norm_mms(mq2, pend_norm[0], pend_norm[1],
                                          pend_norm[2])
            if norm_t_prev:
                pl.wait("scalar", "st", norm_t_prev)
            pl.wait("scalar", "sv", s_q)
            ksq_s = ksq0 if h % 2 == 0 else ksq1
            pl.op("scalar", lambda e, s, h=h, ksq_s=ksq_s: e.activation(
                out=ksq_s[:, :], in_=QT[:, h * T:(h + 1) * T], func=AF.Square),
                inc="ss")
            pend_norm = (h, pl.val("ss"), ksq_s)
        norm_t_prev, _ = norm_mms(mq2, pend_norm[0], pend_norm[1], pend_norm[2])
        qn_done = pl.val("sv")
        pl.wait("sync", "sv", kn_done)
        pl.dma("sync", kvn_in.ap().rearrange("a b -> b a"), kmx[0:1, :], "skn", slow=True)
        pl.wait("gpsimd", "sk0", pl.val("sk0"))
        pl.wait("gpsimd", "sk1", pl.val("sk1"))
        pl.wait("gpsimd", "skn", pl.val("skn"))
        for (i_t, o_t) in ((kvk_in, kvk_out), (kvv_in, kvv_out), (kvn_in, kvn_out)):
            pl.op("gpsimd", lambda e, s, i_t=i_t, o_t=o_t: e.collective_compute(
                "AllGather", ALU.bypass,
                replica_groups=[[0, 1, 2, 3], [4, 5, 6, 7]],
                ins=[i_t.ap().opt()], outs=[o_t.ap().opt()]), inc="cc")
        cc_done = pl.val("cc")
        q_done_s = pl.val("ss")
        qkv_t_last = pl.val("st")
        qkv_evac_last = [evac_tok[H - 2], evac_tok[H - 1]]
        if DEBUG:
            pl.wait("sync", "sv", qn_done)
            pl.dma("sync", dqt, QT[:, :], "dbg")

        gu_cons = [0, 0]
        dw_cons = {}
        hT_done = {}
        silu_tok = {}
        uev_tok = {}
        pl.wait("scalar", "sk0", pl.val("sk0"))
        pl.wait("scalar", "sk1", pl.val("sk1"))
        pl.wait("gpsimd", "st", qkv_t_last)
        pl.wait("sync", "st", qkv_t_last)
        pl.wait("vector", "sk0", pl.val("sk0"))
        pl.wait("vector", "sk1", pl.val("sk1"))
        pair_w = {}
        down_done = {}

        def hts(blkd, ii):
            if ii >= 2:
                return hT[:, ii * T:(ii + 1) * T]
            base = hT if blkd % 2 == 0 else hTx
            return base[:, ii * T:(ii + 1) * T]

        rotd = [pP2[:, 0:512], pP2[:, 512:1024], pR0[:, :], pR1[:, :]]
        rot_tok = [0, 0, 0, 0]
        kdn = [0]

        def emit_down(blkd, lo, hi):
            pl.wait("tensor", "sv", hT_done[blkd * ITB + ITB - 1])
            for idx in range(lo, hi):
                tt, hc = idx // 4, idx % 4
                k = kdn[0]
                ps = rotd[k % 4]
                if rot_tok[k % 4]:
                    pl.wait("tensor", "sv", rot_tok[k % 4])
                for ii in range(ITB):
                    it = blkd * ITB + ii
                    dsl2 = (it // 2) % 3
                    jj2 = it % 2
                    pl.op("tensor",
                          lambda e, s, ps=ps, it=it, ii=ii, tt=tt, hc=hc,
                          dsl2=dsl2, jj2=jj2, blkd=blkd:
                          e.matmul(ps,
                                   hts(blkd, ii)[:, tt * 128:(tt + 1) * 128],
                                   dwb[:, dsl2 * 4096 + jj2 * 2048 + hc * 512:
                                       dsl2 * 4096 + jj2 * 2048 + (hc + 1) * 512],
                                   start=(ii == 0), stop=(ii == ITB - 1)),
                          inc="st" if ii == ITB - 1 else None)
                t_dn = pl.val("st")
                pl.wait("vector", "st", t_dn)
                fsl = facc[:, tt * D + hc * 512: tt * D + (hc + 1) * 512]
                if blkd == 0:
                    pl.op("vector", lambda e, s, fsl=fsl, ps=ps:
                          e.tensor_copy(out=fsl, in_=ps), inc="sv")
                else:
                    pl.op("vector", lambda e, s, fsl=fsl, ps=ps:
                          e.tensor_add(fsl, fsl, ps), inc="sv")
                rot_tok[k % 4] = pl.val("sv")
                kdn[0] += 1
            if hi == 32:
                tok = pl.val("st")
                for ii in range(ITB):
                    dw_cons[blkd * ITB + ii] = tok
                down_done[blkd] = tok

        def ffn_load(pi):
            if pi >= NITB * 2 or pi in pair_w:
                return
            psl_w = pi % 2
            it0 = pi * 2
            gwb2 = gw[:, psl_w * 4096:(psl_w + 1) * 4096]
            uwb2 = uw[:, psl_w * 4096:(psl_w + 1) * 4096]
            if gu_cons[psl_w]:
                pl.wait("sync", "st", gu_cons[psl_w])
            lw = "lw0" if psl_w == 0 else "lw1"
            pl.dma("sync", gwb2.rearrange("p (a c) -> p a c", c=256),
                   wgT[:, it0 * 128:(it0 + 2) * 128]
                   .rearrange("(a p) c -> p a c", p=128), lw)
            pl.dma("sync", uwb2.rearrange("p (a c) -> p a c", c=256),
                   wuT[:, it0 * 128:(it0 + 2) * 128]
                   .rearrange("(a p) c -> p a c", p=128), lw)
            dsl2 = pi % 3
            dwv2 = dwb[:, dsl2 * 4096:(dsl2 + 1) * 4096]
            if pi >= 3:
                pl.wait("sync", "st", dw_cons[(pi - 3) * 2])
            pl.dma("sync", dwv2.rearrange("p (a c) -> p a c", c=2048),
                   wdT[it0 * 128:(it0 + 2) * 128, :]
                   .rearrange("(a p) c -> p a c", p=128), lw)
            t_w = pl.val(lw)
            pl.wait("vector", lw, t_w)
            pl.op("vector", lambda e, s, gwb2=gwb2: e.tensor_scalar(
                out=gwb2, in0=gwb2, scalar1=128.0, scalar2=None,
                op0=ALU.subtract))
            pl.op("vector", lambda e, s, uwb2=uwb2: e.tensor_scalar(
                out=uwb2, in0=uwb2, scalar1=128.0, scalar2=None,
                op0=ALU.subtract))
            v = pl.op("vector", lambda e, s, dwv2=dwv2: e.tensor_scalar(
                out=dwv2, in0=dwv2, scalar1=128.0, scalar2=None,
                op0=ALU.subtract))
            pair_w[pi] = (v, gwb2, uwb2, dwv2)

        ffn_load(0)
        for blk in range(NITB):
            for ii in range(ITB):
                it = blk * ITB + ii
                pi = it // 2
                jj = it % 2
                if blk >= 1 and ii < 2:
                    emit_down(blk - 1, ii * 16, (ii + 1) * 16)
                if it % 2 == 1:
                    ffn_load(pi + 1)
                v_w, gwb2, uwb2, dwv2 = pair_w[pi]
                pl.wait("tensor", "sv", v_w)
                if it >= 1:
                    pl.wait("tensor", "ss", silu_tok[it - 1])
                    pl.wait("tensor", "sv", uev_tok[it - 1])
                else:
                    pl.wait("tensor", "sv", qkv_evac_last[1])
                for (wbv, ps) in ((gwb2, pP0), (uwb2, pP1)):
                    for hc in range(16):
                        for half in range(2):
                            pl.op("tensor",
                                  lambda e, s, wbv=wbv, ps=ps, hc=hc, half=half, jj=jj:
                                  e.matmul(ps[:, half * 512:(half + 1) * 512],
                                           wbv[:, hc * 256 + jj * 128: hc * 256 + (jj + 1) * 128],
                                           xnT[:, hc * T + half * 512: hc * T + (half + 1) * 512],
                                           start=(hc == 0), stop=(hc == 15)),
                                  inc="st" if (hc == 15 and half == 1) else None)
                if jj == 1:
                    gu_cons[pi % 2] = pl.val("st")
                t_gu = pl.val("st")
                gtile = kstg[:, 0:T]
                utile = kstg[:, T:2 * T]
                pl.wait("scalar", "st", t_gu)
                if it >= 1:
                    pl.wait("scalar", "sv", hT_done[it - 1])
                pl.op("scalar", lambda e, s, it=it, gtile=gtile: e.activation(
                    out=gtile, in_=pP0[:, :], func=AF.Silu,
                    scale=gscol[:, it:it + 1]), inc="ss")
                s_g = pl.val("ss")
                silu_tok[it] = s_g
                pl.wait("vector", "st", t_gu)
                pl.op("vector", lambda e, s, it=it, utile=utile: e.tensor_scalar(
                    out=utile, in0=pP1[:, :], scalar1=uscol[:, it:it + 1],
                    scalar2=None, op0=ALU.mult), inc="sv")
                uev_tok[it] = pl.val("sv")
                pl.wait("vector", "ss", s_g)
                if ii < 2 and blk >= 2:
                    pl.wait("vector", "st", down_done[blk - 2])
                pl.op("vector", lambda e, s, ii=ii, blk=blk, gtile=gtile, utile=utile:
                      e.tensor_mul(hts(blk, ii), gtile, utile),
                      inc="sv")
                hT_done[it] = pl.val("sv")
        emit_down(NITB - 1, 0, 32)
        ffn_t_last = pl.val("st")

        pl.wait("vector", "sv", qn_done)
        pl.wait("sync", "cc", cc_done)
        pl.dma("sync", knr[0:1, :], kvn_out[:, :].rearrange("a b -> b a"), "knl", slow=True)
        t_knr = pl.val("knl")
        pl.wait("vector", "knl", t_knr)
        pl.op("vector", lambda e, s: e.reduce_max(
            mk2[0:1, :].rearrange("a (h o) -> a h o", o=1),
            knr[0:1, :].rearrange("a (sh h) -> a h sh", h=H), axis=AXX), inc="sv")
        v_rows = pl.val("sv")
        pl.wait("tensor", "sv", v_rows)
        pl.op("tensor", lambda e, s: e.matmul(
            pR0[:, 0:H], ones_f[:, :], mq2[:, :], start=True, stop=True))
        pl.op("tensor", lambda e, s: e.matmul(
            pR1[:, 0:H], ones_f[:, :], mk2[:, :], start=True, stop=True), inc="st")
        t_rows = pl.val("st")
        pl.wait("vector", "st", t_rows)
        pl.op("vector", lambda e, s: e.tensor_copy(out=csb[:, :], in_=pR1[:, 0:H]))
        pl.op("vector", lambda e, s: e.tensor_mul(csb[:, :], pR0[:, 0:H], csb[:, :]),
              inc="sv")
        pl.wait("scalar", "sv", pl.val("sv"))
        pl.op("scalar", lambda e, s: e.activation(
            out=csb[:, :], in_=csb[:, :], func=AF.Sqrt), inc="ss")
        pl.wait("vector", "ss", pl.val("ss"))
        pl.op("vector", lambda e, s: e.tensor_scalar(
            out=csb[:, :], in0=csb[:, :], scalar1=-RSQ, scalar2=None,
            op0=ALU.mult), inc="sv")
        csb_done = pl.val("sv")
        pl.wait("tensor", "sv", pl.val("sv"))
        dsb_cp = {}
        for j in range(4):
            ps = pR0 if j % 2 == 0 else pR1
            pl.dma("sync", dsrow[:, :],
                   dsc[j * 512:(j + 1) * 512].rearrange("(o a) -> o a", o=1), "ldc")
            t_dsr = pl.val("ldc")
            pl.wait("tensor", "ldc", t_dsr)
            if j >= 2:
                pl.wait("tensor", "sv", dsb_cp[j - 2])
            pl.op("tensor", lambda e, s, j=j, ps=ps: e.matmul(
                ps[:, :], ones_f[:, :], dsrow[:, :],
                start=True, stop=True), inc="st")
            tv = pl.val("st")
            pl.wait("vector", "st", tv)
            dsb_cp[j] = pl.op("vector", lambda e, s, j=j, ps=ps: e.tensor_copy(
                out=dsb[:, j * 512:(j + 1) * 512], in_=ps[:, :]), inc="sv")
            if j < 3:
                pl.wait("sync", "st", tv)
        v_dsb = pl.val("sv")
        fx_done = {}
        pl.wait("vector", "sv", v_dsb)
        pl.wait("sync", "st", ffn_t_last)
        for tt in range(NT):
            xb = xbuf[:, 0:D]
            if tt >= 1:
                pl.wait("sync", "sv", fx_done[tt - 1])
            pl.dma("sync", xb, x_in[tt * 128:(tt + 1) * 128, :], "ldx")
            t_ld = pl.val("ldx")
            pl.wait("vector", "ldx", t_ld)
            fsl = facc[:, tt * D:(tt + 1) * D]
            pl.op("vector", lambda e, s, fsl=fsl: e.tensor_mul(fsl, fsl, dsb[:, :]))
            pl.op("vector", lambda e, s, fsl=fsl, xb=xb: e.tensor_add(fsl, fsl, xb),
                  inc="sv")
            fx_done[tt] = pl.val("sv")
        facc_done = pl.val("sv")
        if DEBUG:
            pl.wait("sync", "sv", csb_done)
            pl.dma("sync", dnr[0:1, 0:H], mq2[0:1, :], "dbg")
            pl.dma("sync", dnr[0:1, H:2 * H], mk2[0:1, :], "dbg")
            pl.dma("sync", dnr[0:1, 2 * H:3 * H], csb[0:1, :], "dbg")
            pl.dma("sync", dnr[0:1, 3 * H:3 * H + SHD * H], knr[0:1, :], "dbg")
            pl.dma("sync", dcs, csb[:, :], "dbg")
            pl.dma("sync", dkk, kvk_out[:, :], "dbg")
            pl.dma("sync", dkv, kvv_out[:, :], "dbg")
            pl.dma("sync", dkn, kvn_out[:, :], "dbg")
        pl.wait("scalar", "sv", csb_done)
        pl.wait("tensor", "sv", facc_done if False else v_dsb)
        pl.wait("tensor", "sv", csb_done)

        kv_cons = {}
        exp_done = {}
        av_done = {}
        s_done = {}
        rbs_tok = {}
        norm_tok = {}

        def head_finish(h):
            emit_avl(h, 31)
            t_av = av_done[31]
            pl.wait("vector", "st", t_av)
            pl.op("vector", lambda e, s: e.reciprocal(rrow[0:1, 0:512], pR0[0:1, :]))
            pl.op("vector", lambda e, s: e.reciprocal(rrow[0:1, 512:1024], pR1[0:1, :]),
                  inc="sv")
            v_r = pl.val("sv")
            pl.wait("tensor", "sv", v_r)
            pl.op("tensor", lambda e, s: e.matmul(
                pR0[:, :], ones_f[:, :], rrow[0:1, 0:512], start=True, stop=True))
            pl.op("tensor", lambda e, s: e.matmul(
                pR1[:, :], ones_f[:, :], rrow[0:1, 512:1024], start=True, stop=True),
                inc="st")
            t_rb = pl.val("st")
            pl.wait("vector", "st", t_rb)
            pl.op("vector", lambda e, s: e.tensor_copy(out=rbs[:, 0:512], in_=pR0[:, :]))
            pl.op("vector", lambda e, s: e.tensor_copy(out=rbs[:, 512:1024], in_=pR1[:, :]),
                  inc="sv")
            rbs_tok[h] = pl.val("sv")
            pl.op("vector", lambda e, s, h=h: e.tensor_mul(
                aoT[:, h * T:(h + 1) * T], pP2[:, :], rbs[:, :]), inc="sv")
            norm_tok[h] = pl.val("sv")

        def emit_avl(h, gkc):
            vbp = vbf[:, (gkc // 8 % 2) * T:((gkc // 8 % 2) + 1) * T]
            kc = gkc % 8
            pslot = (gkc % 2) * T
            pl.wait("tensor", "ss", exp_done[gkc])
            if gkc == 0 and h >= 1:
                pl.wait("tensor", "sv", rbs_tok[h - 1])
                pl.wait("tensor", "sv", norm_tok[h - 1])
            for half in range(2):
                pl.op("tensor",
                      lambda e, s, vbp=vbp, kc=kc, half=half, pslot=pslot, gkc=gkc:
                      e.matmul(pP2[:, half * 512:(half + 1) * 512],
                               vbp[:, kc * 128:(kc + 1) * 128],
                               ptb[:, pslot + half * 512: pslot + (half + 1) * 512],
                               start=(gkc == 0), stop=(gkc == 31)))
                pl.op("tensor", lambda e, s, half=half, pslot=pslot, gkc=gkc:
                      e.matmul((pR0 if half == 0 else pR1)[0:1, :],
                               ones_b[:, :],
                               ptb[:, pslot + half * 512: pslot + (half + 1) * 512],
                               start=(gkc == 0), stop=(gkc == 31)),
                      inc="st" if half == 1 else None)
            av_done[gkc] = pl.val("st")
            if gkc % 8 == 7:
                kv_cons[(h, gkc // 8)] = av_done[gkc]
        if DEBUG:
            pl.wait("vector", "dbg", pl.val("dbg"))
        pl.wait("sync", "st", ffn_t_last)
        for h in range(H):
            for sh in range(SHD):
                bsl = sh % 2
                kb = kbuf[:, bsl * T:(bsl + 1) * T]
                vb = vbf[:, bsl * T:(bsl + 1) * T]
                prev = (h, sh - 2) if sh >= 2 else ((h - 1, sh + 2) if h >= 1 else None)
                if prev in kv_cons:
                    pl.wait("sync", "st", kv_cons[prev])
                kvs = "kv0" if bsl == 0 else "kv1"
                pl.dma("sync", kb,
                       kvk_out[(sh * H + h) * DH:(sh * H + h + 1) * DH, :], kvs)
                pl.dma("sync", vb.rearrange("p (j d) -> p j d", d=DH),
                       kvv_out[(sh * H + h) * T:(sh * H + h + 1) * T, :]
                       .rearrange("(j p) d -> p j d", p=128), kvs)
                t_kv = pl.val(kvs)
                pl.wait("tensor", kvs, t_kv)
                for kc in range(8):
                    gkc = sh * 8 + kc
                    psl = pP0 if gkc % 2 == 0 else pP1
                    pslot = (gkc % 2) * T
                    if gkc >= 2:
                        pl.wait("tensor", "ss", exp_done[gkc - 2])
                    elif h >= 1 and gkc == 0:
                        pl.wait("tensor", "ss", exp_done[30])
                    elif h >= 1 and gkc == 1:
                        pl.wait("tensor", "ss", exp_done[31])
                    for half in range(2):
                        pl.op("tensor",
                              lambda e, s, kb=kb, kc=kc, h=h, half=half, psl=psl:
                              e.matmul(psl[:, half * 512:(half + 1) * 512],
                                       kb[:, kc * 128:(kc + 1) * 128],
                                       QT[:, h * T + half * 512: h * T + (half + 1) * 512],
                                       start=True, stop=True),
                              inc="st" if half == 1 else None)
                    s_done[gkc] = pl.val("st")
                    pl.wait("scalar", "st", s_done[gkc])
                    if gkc >= 2:
                        pl.wait("scalar", "st", av_done[gkc - 2])
                    pl.op("scalar", lambda e, s, psl=psl, pslot=pslot, h=h:
                          e.activation(out=ptb[:, pslot:pslot + T], in_=psl[:, :],
                                       func=AF.Exp, bias=csb[:, h:h + 1],
                                       scale=RSQ), inc="ss")
                    exp_done[gkc] = pl.val("ss")
                    if gkc >= 1:
                        emit_avl(h, gkc - 1)
                    elif h >= 1:
                        head_finish(h - 1)
        head_finish(H - 1)
        attn_done = pl.val("sv")
        if DEBUG:
            pl.wait("sync", "sv", attn_done)
            pl.dma("sync", dao, aoT[:, :], "dbg")
            pl.wait("sync", "sv", facc_done)
            pl.dma("sync", dfa, facc[:, :], "dbg")

        pl.wait("tensor", "sv", attn_done)
        pl.wait("tensor", "sv", facc_done)
        pl.wait("sync", "sv", attn_done)
        wo_cons = [0, 0]
        wo_tt = 0
        ost_tok = {}
        attn_t_last = pl.val("st")
        pl.wait("gpsimd", "st", attn_t_last)
        pl.wait("gpsimd", "sv", facc_done)
        pl.wait("sync", "st", attn_t_last)
        pl.wait("sync", "sv", facc_done)
        woscl = dAr[:, 0:8192]
        rot = [pP2[:, 0:512], pP2[:, 512:1024], pP0[:, 0:512], pP0[:, 512:1024]]
        rot_tok = [0, 0, 0, 0]
        k = 0
        bo_mm_tok = 0
        wo_tt_l = [0]
        wo_state = {}

        def wo_load(dc2):
            if dc2 >= 4 or dc2 in wo_state:
                return
            wsl2 = dc2 % 2
            w0 = woc[:, wsl2 * 8192:(wsl2 + 1) * 8192]
            lws = "lw0" if wsl2 == 0 else "lw1"
            if wo_cons[wsl2]:
                pl.wait("sync", "st", wo_cons[wsl2])
            pl.dma("sync", w0.rearrange("p (a u c) -> p a u c", u=4, c=128),
                   co[:, dc2 * 512:(dc2 + 1) * 512]
                   .rearrange("(a p) (u c) -> p a u c", p=128, c=128), lws)
            if wo_tt_l[0]:
                pl.wait("sync", "sv", wo_tt_l[0])
            pl.dma("sync", woscl.rearrange("p (a u c) -> p a u c", u=4, c=128),
                   so[:, dc2 * 512:(dc2 + 1) * 512]
                   .rearrange("(a p) (u c) -> p a u c", p=128, c=128), lws)
            t_wo = pl.val(lws)
            pl.wait("vector", lws, t_wo)
            v = pl.op("vector", lambda e, s, w0=w0: e.tensor_mul(w0, w0, woscl))
            wo_tt_l[0] = v
            wo_state[dc2] = (w0, v)

        wo_load(0)
        for dc in range(4):
            if bo_mm_tok:
                pl.wait("sync", "st", bo_mm_tok)
            pl.dma("sync", bor[:, :],
                   bo[dc * 512:(dc + 1) * 512].rearrange("(o a) -> o a", o=1), "ldc")
            t_bor = pl.val("ldc")
            pl.wait("tensor", "ldc", t_bor)
            wsl = dc % 2
            w0, v_wo = wo_state[dc]
            pl.wait("tensor", "sv", v_wo)
            wo_load(dc + 1)
            for tt in range(NT):
                ps = rot[k % 4]
                if rot_tok[k % 4]:
                    pl.wait("tensor", "sv", rot_tok[k % 4])
                for fc in range(16):
                    pl.op("tensor", lambda e, s, ps=ps, fc=fc, tt=tt, w0=w0:
                          e.matmul(ps,
                                   aoT[:, fc * T + tt * 128: fc * T + (tt + 1) * 128],
                                   w0[:, fc * 512:(fc + 1) * 512],
                                   start=(fc == 0), stop=False))
                pl.op("tensor", lambda e, s, ps=ps: e.matmul(
                    ps, ones_f[:, :], bor[0:1, :],
                    start=False, stop=True), inc="st")
                t_op = pl.val("st")
                bo_mm_tok = t_op
                osl = outs[:, (k % 2) * 512:(k % 2 + 1) * 512]
                if k >= 2:
                    pl.wait("vector", "st0" if k % 2 == 0 else "st1",
                            ost_tok[k - 2])
                pl.wait("vector", "st", t_op)
                pl.op("vector", lambda e, s, ps=ps, osl=osl, tt=tt, dc=dc:
                      e.tensor_add(osl, ps,
                                   facc[:, tt * D + dc * 512: tt * D + (dc + 1) * 512]),
                      inc="sv")
                rot_tok[k % 4] = pl.val("sv")
                pl.wait("sync", "sv", rot_tok[k % 4])
                ost_tok[k] = pl.dma(
                    "sync",
                    y_out[tt * 128:(tt + 1) * 128, dc * 512:(dc + 1) * 512],
                    osl, "st0" if k % 2 == 0 else "st1")
                k += 1
            wo_cons[wsl] = pl.val("st")
        pl.wait("sync", "st0", pl.val("st0"))
        pl.wait("sync", "st1", pl.val("st1"))

        block = ctx.enter_context(nc.Block())
        S_ = sems

        @block.sync
        def _(e):
            for fn in pl.q["sync"]:
                fn(e, S_)

        @block.gpsimd
        def _(e):
            for fn in pl.q["gpsimd"]:
                fn(e, S_)

        @block.vector
        def _(e):
            for fn in pl.q["vector"]:
                fn(e, S_)

        @block.scalar
        def _(e):
            for fn in pl.q["scalar"]:
                fn(e, S_)

        @block.tensor
        def _(e):
            for fn in pl.q["tensor"]:
                fn(e, S_)

    return nc


_NC_CACHE = None


def kernel(**inputs):
    global _NC_CACHE
    x = np.asarray(inputs["L0_attn_x"])
    x2d = np.ascontiguousarray(x.reshape(B * S, D))
    bf = ml_dtypes.bfloat16
    rep = lambda s: np.ascontiguousarray(
        np.repeat(np.asarray(s), 32, axis=0).astype(bf))
    base = {
        "cq": np.ascontiguousarray(np.asarray(inputs["L0_attn_wq_codes"]).astype(bf)),
        "ck": np.ascontiguousarray(np.asarray(inputs["L0_attn_wk_codes"]).astype(bf)),
        "cv": np.ascontiguousarray(np.asarray(inputs["L0_attn_wv_codes"]).astype(bf)),
        "co": np.ascontiguousarray(np.asarray(inputs["L0_attn_wo_codes"]).astype(bf)),
        "sq": rep(inputs["L0_attn_wq_scales"]),
        "sk": rep(inputs["L0_attn_wk_scales"]),
        "svs": rep(inputs["L0_attn_wv_scales"]),
        "so": rep(inputs["L0_attn_wo_scales"]),
        "bq": np.ascontiguousarray(np.asarray(inputs["L0_attn_bq"])),
        "bqcd": np.ascontiguousarray(np.asarray(inputs["L0_attn_bq"]).reshape(16, 128).T),
        "bkcd": np.ascontiguousarray(np.asarray(inputs["L0_attn_bk"]).reshape(16, 128).T),
        "bvcd": np.ascontiguousarray(np.asarray(inputs["L0_attn_bv"]).reshape(16, 128).T),
        "gscd": np.ascontiguousarray(np.asarray(inputs["L0_ffn_gate_scale"]).reshape(64, 128).T),
        "uscd": np.ascontiguousarray(np.asarray(inputs["L0_ffn_up_scale"]).reshape(64, 128).T),
        "bk": np.ascontiguousarray(np.asarray(inputs["L0_attn_bk"])),
        "bv": np.ascontiguousarray(np.asarray(inputs["L0_attn_bv"])),
        "bo": np.ascontiguousarray(np.asarray(inputs["L0_attn_bo"])),
        "wgT": np.ascontiguousarray(np.asarray(inputs["L0_ffn_gate_w"]).T.astype(bf)),
        "wuT": np.ascontiguousarray(np.asarray(inputs["L0_ffn_up_w"]).T.astype(bf)),
        "wdT": np.ascontiguousarray(np.asarray(inputs["L0_ffn_down_w"]).T.astype(bf)),
        "gsc": np.ascontiguousarray(np.asarray(inputs["L0_ffn_gate_scale"])),
        "usc": np.ascontiguousarray(np.asarray(inputs["L0_ffn_up_scale"])),
        "dsc": np.ascontiguousarray(np.asarray(inputs["L0_ffn_down_scale"])),
    }
    in_maps = []
    for c in range(NCORES):
        m = dict(base)
        m["x"] = np.ascontiguousarray(x2d[c * T:(c + 1) * T, :])
        in_maps.append(m)
    if _NC_CACHE is None:
        _NC_CACHE = build_nc()
    res = run_bass_kernel_spmd(_NC_CACHE, in_maps, core_ids=list(range(NCORES)))
    out = np.concatenate([np.asarray(r["y"]) for r in res.results], axis=0)
    return out.reshape(B, S, D)
